# version 01
# baseline (speedup 1.0000x reference)
import contextlib
import numpy as np
import ml_dtypes
import concourse.bass as bass
import concourse.mybir as mybir
from concourse.bass_utils import run_bass_kernel_spmd

F32 = mybir.dt.float32
BF16 = mybir.dt.bfloat16
AF = mybir.ActivationFunctionType
ALU = mybir.AluOpType
AX = mybir.AxisListType

BF = ml_dtypes.bfloat16

B, L, D = 2, 8192, 2048
NCORES = 8
TOK = (B * L) // NCORES
EPS = 1e-5



def cdT(x):
    return np.ascontiguousarray(np.asarray(x).T)


def cd(x):
    return np.ascontiguousarray(np.asarray(x))


def tile128(v):
    v = np.asarray(v)
    return np.ascontiguousarray(v.reshape(-1, 128).T)


def stream_lhsT(WT):
    K, M = WT.shape
    KC, MT = K // 128, M // 128
    return np.ascontiguousarray(
        WT.reshape(KC, 128, MT, 128).transpose(2, 1, 0, 3).reshape(MT, 128, KC * 128))



ENGS = ("pe", "ve", "se", "gp", "sy")
_ENG_ATTR = {"pe": "tensor", "ve": "vector", "se": "scalar", "gp": "gpsimd", "sy": "sync"}


class Prog:
    def __init__(self, nc, stack):
        self.nc = nc
        self.q = {e: [] for e in ENGS}
        self.cnt = {e: 0 for e in ENGS}
        self.marked = {e: True for e in ENGS}
        self.dcnt = {e: 0 for e in ENGS}
        self.sem_c = {e: stack.enter_context(nc.semaphore(f"sc_{e}")) for e in ENGS}
        self.sem_d = {e: stack.enter_context(nc.semaphore(f"sd_{e}"))
                      for e in ("sy", "gp", "se", "ve")}
        self.waited = {}

    def op(self, e, fn):
        self.q[e].append([fn, "op"])
        self.marked[e] = False

    def pe(self, fn): self.op("pe", fn)
    def ve(self, fn): self.op("ve", fn)
    def se(self, fn): self.op("se", fn)
    def gp(self, fn): self.op("gp", fn)
    def sy(self, fn): self.op("sy", fn)

    def dma(self, e, out, in_, **kw):
        sem = self.sem_d[e]
        def fn(eng, out=out, in_=in_, kw=kw, sem=sem):
            return eng.dma_start(out=out, in_=in_, **kw).then_inc(sem, 16)
        self.q[e].append([fn, "dma"])
        self.dcnt[e] += 16
        return self.dcnt[e]

    def mark(self, e):
        if self.marked[e]:
            return self.cnt[e]
        sem = self.sem_c[e]
        self.cnt[e] += 1
        def fn(eng, sem=sem):
            return eng.drain(fusable=False).then_inc(sem, 1)
        self.q[e].append([fn, "marked"])
        self.marked[e] = True
        return self.cnt[e]

    def wait(self, cons, e, val):
        if val <= 0:
            return
        key = (cons, "c", e)
        if self.waited.get(key, 0) >= val:
            return
        sem = self.sem_c[e]
        self.q[cons].append([lambda eng, sem=sem, val=val: eng.wait_ge(sem, val), "wait"])
        self.waited[key] = val

    def wait_dma(self, cons, e, val):
        if val <= 0:
            return
        key = (cons, "d", e)
        if self.waited.get(key, 0) >= val:
            return
        sem = self.sem_d[e]
        self.q[cons].append([lambda eng, sem=sem, val=val: eng.wait_ge(sem, val), "wait"])
        self.waited[key] = val

    def fence(self, prod, *cons):
        v = self.mark(prod)
        for c in cons:
            self.wait(c, prod, v)
        return v

    def fence_dma(self, e, *cons):
        v = self.dcnt[e]
        for c in cons:
            self.wait_dma(c, e, v)
        return v

    def emit(self, blk):
        for e in ENGS:
            ops = self.q[e]
            if not ops:
                continue
            def run(eng, ops=ops):
                for rec in ops:
                    rec[0](eng)
            getattr(blk, _ENG_ATTR[e])(run)


class Nc:
    def __init__(self):
        self.nc = bass.Bass("TRN2", target_bir_lowering=False)
        self.stack = contextlib.ExitStack()
        self.p = None

    def inp(self, name, shape, dt=F32):
        return self.nc.dram_tensor(name, list(shape), dt, kind="ExternalInput").ap()

    def out(self, name, shape, dt=F32):
        return self.nc.dram_tensor(name, list(shape), dt, kind="ExternalOutput").ap()

    def scratch(self, name, shape, dt=F32):
        return self.nc.dram_tensor(name, list(shape), dt).ap()

    def sb(self, name, shape, dt=F32):
        return self.stack.enter_context(self.nc.sbuf_tensor(name, list(shape), dt))

    def ps(self, name, shape, dt=F32):
        return self.stack.enter_context(self.nc.psum_tensor(name, list(shape), dt))

    def start(self):
        self.p = Prog(self.nc, self.stack)
        return self.p

    def finish(self):
        blk = self.stack.enter_context(self.nc.Block())
        self.p.emit(blk)
        self.stack.close()
        return self.nc


def free_bcast(tile_ap, reps, tt):
    a = tile_ap
    return bass.AP(a.tensor, a.offset, [a.ap[0], [0, reps], [1, tt]])


_launch_count = 0


def run_launch(nc, in_maps, **kw):
    global _launch_count
    _launch_count += 1
    return run_bass_kernel_spmd(nc, in_maps, core_ids=list(range(len(in_maps))), **kw)


def make_consts(b, p, values):
    vals = sorted(set(float(v) for v in values))
    cst = b.sb("cstpool", [128, max(1, len(vals))])
    for i, v in enumerate(vals):
        p.ve(lambda ve, i=i, v=v: ve.memset(cst[:, i:i + 1], v))
    p.fence("ve", "se", "pe", "gp")
    return {v: cst[:, i:i + 1] for i, v in enumerate(vals)}



def emit_ln_rows(p, x_t, sq, KC, tt, onesf, ps_misc, rows, csts):
    p.se(lambda s: s.activation(out=sq[:, :, :], in_=x_t[:, :, :], func=AF.Square))
    p.fence("se", "pe")
    for k in range(KC):
        p.pe(lambda t, k=k: t.matmul(ps_misc[0:1, 0:tt], onesf[:, :], x_t[:, k, :],
                                     start=(k == 0), stop=(k == KC - 1)))
    for k in range(KC):
        p.pe(lambda t, k=k: t.matmul(ps_misc[32:33, 0:tt], onesf[:, :], sq[:, k, :],
                                     start=(k == 0), stop=(k == KC - 1)))
    p.fence("pe", "se", "ve")
    inv_n = 1.0 / (KC * 128)
    p.se(lambda s: s.mul(rows[0:1, 0, :], ps_misc[0:1, 0:tt], inv_n))
    p.se(lambda s: s.activation(out=rows[0:1, 2, :], in_=rows[0:1, 0, :], func=AF.Square))
    p.fence("se", "ve")
    p.ve(lambda v: v.scalar_tensor_tensor(
        out=rows[0:1, 3, :], in0=ps_misc[32:33, 0:tt], scalar=inv_n,
        in1=rows[0:1, 2, :], op0=ALU.mult, op1=ALU.subtract))
    p.fence("ve", "se")
    p.se(lambda s: s.activation(out=rows[0:1, 3, :], in_=rows[0:1, 3, :], func=AF.Ln,
                                bias=csts[EPS][0:1, :]))
    p.se(lambda s: s.activation(out=rows[0:1, 1, :], in_=rows[0:1, 3, :], func=AF.Exp,
                                scale=csts[-0.5][0:1, :]))



def build_ffn0(tok=TOK, tt=512):
    KC = D // 128
    MH = 4096 // 128
    MO = D // 128
    NT = tok // tt

    b = Nc()
    xT = b.inp("xT", [D, tok])
    wg = b.inp("wg", [D, 2])
    w0a = b.inp("w0a", [MH, 128, KC * 128], BF16)
    w1a = b.inp("w1a", [MH, 128, KC * 128], BF16)
    w0b = b.inp("w0b", [MO, 128, MH * 128], BF16)
    w1b = b.inp("w1b", [MO, 128, MH * 128], BF16)
    b0a = b.inp("b0a", [128, MH])
    b1a = b.inp("b1a", [128, MH])
    b0b = b.inp("b0b", [128, MO])
    b1b = b.inp("b1b", [128, MO])
    onesf_d = b.inp("onesf", [128, 1])
    ones_r_d = b.inp("ones_r", [1, 128])
    pm1_d = b.inp("pm1", [2, 1])
    yT = b.out("yT", [D, tok])

    xTv = xT.rearrange("(c p) t -> p c t", p=128)
    yTv = yT.rearrange("(c p) t -> p c t", p=128)
    wgv = wg.rearrange("(c p) m -> p c m", p=128)

    x_t = b.sb("x_t", [128, KC, tt])
    sq = b.sb("sq", [128, KC, tt])
    u = b.sb("u", [128, KC, tt], BF16)
    h0 = b.sb("h0", [128, MH, tt], BF16)
    h1 = b.sb("h1", [128, MH, tt], BF16)
    rows = b.sb("rows", [1, 6, tt])
    grow = b.sb("grow", [2, tt])
    c1 = b.sb("c1", [128, tt])
    dscr = b.sb("dscr", [128, tt])
    res = b.sb("res", [128, 2, tt])
    onesf = b.sb("onesf_sb", [128, 1])
    ones_r = b.sb("ones_r_sb", [1, 128])
    pm1 = b.sb("pm1_sb", [2, 1])
    wg_sb = b.sb("wg_sb", [128, KC, 2])
    b0a_sb = b.sb("b0a_sb", [128, MH])
    b1a_sb = b.sb("b1a_sb", [128, MH])
    b0b_sb = b.sb("b0b_sb", [128, MO])
    b1b_sb = b.sb("b1b_sb", [128, MO])
    gsd = b.sb("gsd", [1, 1])
    wA = b.sb("wA", [128, 2, KC * 128], BF16)
    wB0 = b.sb("wB0", [128, 2, MH * 128], BF16)
    wB1 = b.sb("wB1", [128, 2, MH * 128], BF16)

    ps_h = [b.ps("ps_h0", [128, tt]), b.ps("ps_h1", [128, tt])]
    ps_o = [b.ps("ps_o0", [128, tt]), b.ps("ps_o1", [128, tt])]
    ps_b = [b.ps("ps_b0", [128, tt]), b.ps("ps_b1", [128, tt])]
    ps_misc = b.ps("ps_misc", [128, tt])

    p = b.start()
    csts = make_consts(b, p, [EPS, -0.5])

    p.dma("sy", onesf[:, :], onesf_d)
    p.dma("sy", ones_r[:, :], ones_r_d)
    p.dma("sy", pm1[:, :], pm1_d)
    p.dma("sy", wg_sb[:, :, :], wgv)
    p.dma("sy", b0a_sb[:, :], b0a)
    p.dma("sy", b1a_sb[:, :], b1a)
    p.dma("sy", b0b_sb[:, :], b0b)
    p.dma("sy", b1b_sb[:, :], b1b)
    p.fence_dma("sy", "pe", "ve", "se")

    for k in range(KC):
        p.pe(lambda t, k=k: t.matmul(ps_misc[64:66, 0:1], wg_sb[:, k, :], onesf[:, :],
                                     start=(k == 0), stop=(k == KC - 1)))
    p.fence("pe", "ve")
    p.ve(lambda v: v.tensor_copy(grow[0:2, 0:1], ps_misc[64:66, 0:1]))
    p.fence("ve", "pe")
    p.pe(lambda t: t.matmul(ps_misc[0:1, 0:1], pm1[:, :], grow[0:2, 0:1], start=True, stop=True))
    p.fence("pe", "ve")
    p.ve(lambda v: v.tensor_copy(gsd[:, :], ps_misc[0:1, 0:1]))
    p.fence("ve", "pe")

    h_evac = [0, 0]
    o_evac_ve = 0
    o_evac_se = 0
    wA_use = [0, 0]
    wB_use = [0, 0]
    res_store = [0, 0]

    for it in range(NT):
        t0 = it * tt
        if it > 0:
            p.fence("ve", "sy")
        p.dma("sy", x_t[:, :, :], xTv[:, :, t0:t0 + tt])
        p.fence_dma("sy", "se", "pe", "ve")
        if it > 0:
            p.fence("ve", "pe")

        emit_ln_rows(p, x_t, sq, KC, tt, onesf, ps_misc, rows, csts)
        p.fence("se", "pe", "ve")

        for k in range(KC):
            p.pe(lambda t, k=k: t.matmul(ps_misc[64:66, 0:tt], wg_sb[:, k, :], x_t[:, k, :],
                                         start=(k == 0), stop=(k == KC - 1)))
        p.fence("pe", "ve")
        p.ve(lambda v: v.tensor_copy(grow[:, :], ps_misc[64:66, 0:tt]))
        p.fence("ve", "pe")
        p.pe(lambda t: t.matmul(ps_misc[0:1, 0:tt], pm1[:, :], grow[:, :], start=True, stop=True))
        p.fence("pe", "ve")
        p.ve(lambda v: v.scalar_tensor_tensor(
            out=rows[0:1, 4, :], in0=rows[0:1, 0, :], scalar=gsd[0:1, 0:1],
            in1=ps_misc[0:1, 0:tt], op0=ALU.mult, op1=ALU.subtract))
        p.ve(lambda v: v.tensor_single_scalar(
            out=rows[0:1, 5, :], in_=rows[0:1, 4, :], scalar=0.0, op=ALU.is_le))
        mask_ready = p.fence("ve", "pe")

        p.pe(lambda t: t.matmul(ps_b[0][:, :], ones_r[:, :], rows[0:1, 0, :], start=True, stop=True))
        p.pe(lambda t: t.matmul(ps_b[1][:, :], ones_r[:, :], rows[0:1, 1, :], start=True, stop=True))
        p.fence("pe", "ve")

        mean_b = free_bcast(ps_b[0][:, 0:1], KC, tt)
        rstd_b = free_bcast(ps_b[1][:, 0:1], KC, tt)
        p.ve(lambda v: v.tensor_sub(sq[:, :, :], x_t[:, :, :], mean_b))
        p.ve(lambda v: v.tensor_mul(u[:, :, :], sq[:, :, :], rstd_b))
        p.fence("ve", "pe", "gp")

        p.pe(lambda t: t.matmul(ps_b[0][:, :], ones_r[:, :], rows[0:1, 5, :], start=True, stop=True))
        p.mark("pe")

        for e in range(2):
            wsrc = w0a if e == 0 else w1a
            hdst = h0 if e == 0 else h1
            bias = b0a_sb if e == 0 else b1a_sb
            act = AF.Gelu if e == 0 else AF.Relu
            for m in range(MH):
                gi = e * MH + m
                buf = gi % 2
                p.wait("gp", "pe", wA_use[buf])
                ldv = p.dma("gp", wA[:, buf, :], wsrc[m])
                p.wait_dma("pe", "gp", ldv)
                pb = gi % 2
                p.wait("pe", "se", h_evac[pb])
                for k in range(KC):
                    p.pe(lambda t, k=k, pb=pb, buf=buf: t.matmul(
                        ps_h[pb][:, :], wA[:, buf, k * 128:(k + 1) * 128], u[:, k, :],
                        start=(k == 0), stop=(k == KC - 1)))
                wA_use[buf] = p.fence("pe", "se")
                p.se(lambda s, m=m, pb=pb, hdst=hdst, bias=bias, act=act: s.activation(
                    out=hdst[:, m, :], in_=ps_h[pb][:, :], func=act, bias=bias[:, m:m + 1]))
                h_evac[pb] = p.mark("se")
        p.fence("se", "pe")

        for mo in range(MO):
            buf = mo % 2
            p.wait("gp", "pe", wB_use[buf])
            p.dma("gp", wB0[:, buf, :], w0b[mo])
            ldv = p.dma("gp", wB1[:, buf, :], w1b[mo])
            p.wait_dma("pe", "gp", ldv)
            p.wait("pe", "ve", o_evac_ve)
            p.wait("pe", "se", o_evac_se)
            for k in range(MH):
                p.pe(lambda t, k=k, buf=buf: t.matmul(
                    ps_o[0][:, :], wB0[:, buf, k * 128:(k + 1) * 128], h0[:, k, :],
                    start=(k == 0), stop=(k == MH - 1)))
            for k in range(MH):
                p.pe(lambda t, k=k, buf=buf: t.matmul(
                    ps_o[1][:, :], wB1[:, buf, k * 128:(k + 1) * 128], h1[:, k, :],
                    start=(k == 0), stop=(k == MH - 1)))
            wB_use[buf] = p.fence("pe", "se", "ve")
            p.wait("se", "ve", o_evac_ve)
            p.se(lambda s, mo=mo: s.activation(out=c1[:, :], in_=ps_o[1][:, :],
                                               func=AF.Identity, bias=b1b_sb[:, mo:mo + 1]))
            o_evac_se = p.fence("se", "ve")
            p.ve(lambda v, mo=mo: v.scalar_tensor_tensor(
                out=dscr[:, :], in0=ps_o[0][:, :], scalar=b0b_sb[:, mo:mo + 1],
                in1=c1[:, :], op0=ALU.add, op1=ALU.subtract))
            p.ve(lambda v: v.tensor_mul(dscr[:, :], dscr[:, :], ps_b[0][:, :]))
            p.ve(lambda v: v.tensor_add(dscr[:, :], dscr[:, :], c1[:, :]))
            p.wait_dma("ve", "sy", res_store[buf])
            p.ve(lambda v, mo=mo, buf=buf: v.scalar_tensor_tensor(
                out=res[:, buf, :], in0=dscr[:, :], scalar=0.5,
                in1=x_t[:, mo, :], op0=ALU.mult, op1=ALU.add))
            o_evac_ve = p.fence("ve", "sy")
            res_store[buf] = p.dma("sy", yTv[:, mo, t0:t0 + tt], res[:, buf, :])

    p.fence_dma("sy", "sy")
    return b.finish()


def host_ffn0(inputs, bw, prefix):
    g = lambda k: np.asarray(inputs[prefix + k])
    return dict(
        wg=cdT(g("Wg")),
        w0a=bw[prefix + "W0a"], w1a=bw[prefix + "W1a"],
        w0b=bw[prefix + "W0b"], w1b=bw[prefix + "W1b"],
        b0a=tile128(g("b0a")), b1a=tile128(g("b1a")),
        b0b=tile128(g("b0b")), b1b=tile128(g("b1b")),
        onesf=np.ones((128, 1), np.float32),
        ones_r=np.ones((1, 128), np.float32),
        pm1=np.array([[1.0], [-1.0]], np.float32),
    )


def shard_tokens(x):
    out = []
    for c in range(NCORES):
        bidx = c // (NCORES // B)
        s = (c % (NCORES // B)) * TOK
        out.append(cdT(x[bidx, s:s + TOK, :]))
    return out


def unshard_tokens(parts):
    x = np.empty((B, L, D), np.float32)
    for c in range(NCORES):
        bidx = c // (NCORES // B)
        s = (c % (NCORES // B)) * TOK
        x[bidx, s:s + TOK, :] = parts[c].T
    return x




PREP_TILE = 128 * 16384


def build_prep2(nblob_tiles):
    b = Nc()
    blob = b.inp("blob", [nblob_tiles * PREP_TILE])
    w2Ts = b.inp("w2Ts", [8192, 256])
    h128 = b.inp("h128", [128, 128], BF16)
    h64 = b.inp("h64", [64, 64], BF16)
    blob_o = b.out("blob_o", [nblob_tiles * PREP_TILE], BF16)
    w2p = b.out("w2p", [8192, 256], BF16)
    waT = b.scratch("waT", [8192, 256], BF16)

    blv = blob.rearrange("(n p t) -> n p t", p=128, t=16384)
    bov = blob_o.rearrange("(n p t) -> n p t", p=128, t=16384)
    w2v = w2Ts.rearrange("(f p) d -> f p d", p=128)
    wav = waT.rearrange("(f p) d -> f p d", p=128)

    cb = b.sb("cb", [128, 2, 16384], BF16)
    h128_sb = b.sb("h128_sb", [128, 128], BF16)
    h64_sb = b.sb("h64_sb", [64, 64], BF16)
    wblk = b.sb("wblk", [128, 2, 256], BF16)
    wa_sb = b.sb("wa_sb", [128, 2, 256], BF16)
    comb = b.sb("comb", [64, 2, 256], BF16)
    w2p_sb = b.sb("w2p_sb", [64, 2, 256], BF16)
    ps = b.ps("ps", [128, 2, 256])

    p = b.start()
    csts = make_consts(b, p, [2.0 ** -6.5])
    p.dma("sy", h128_sb[:, :], h128)
    p.dma("sy", h64_sb[:, :], h64)

    st_done = [0, 0]
    for i in range(nblob_tiles):
        bi = i % 2
        p.wait_dma("gp", "sy", st_done[bi])
        ld = p.dma("gp", cb[:, bi, :], blv[i])
        p.wait_dma("sy", "gp", ld)
        st_done[bi] = p.dma("sy", bov[i], cb[:, bi, :])

    evacd = [0, 0]
    stored = [0, 0]
    for f in range(64):
        bi = f % 2
        p.wait("gp", "ve", evacd[bi])
        ld = p.dma("gp", wblk[:, bi, :], w2v[f])
        p.wait_dma("pe", "gp", ld)
        p.wait("pe", "ve", evacd[bi])
        p.pe(lambda t, bi=bi: t.matmul(ps[:, bi, :], h128_sb[:, :], wblk[:, bi, :],
                                       start=True, stop=True))
        p.fence("pe", "ve")
        p.wait_dma("ve", "sy", stored[bi])
        p.ve(lambda v, bi=bi: v.tensor_copy(wa_sb[:, bi, :], ps[:, bi, :]))
        evacd[bi] = p.fence("ve", "sy", "pe", "gp")
        stored[bi] = p.dma("sy", wav[f], wa_sb[:, bi, :])
    p.fence_dma("sy", "gp")

    evacd = [0, 0]
    stored = [0, 0]
    for g in range(128):
        bi = g % 2
        src = bass.AP(waT.tensor, g * 256, [[128 * 256, 64], [1, 256]])
        p.wait("gp", "se", evacd[bi])
        ld = p.dma("gp", comb[:, bi, :], src)
        p.wait_dma("pe", "gp", ld)
        p.wait("pe", "se", evacd[bi])
        p.pe(lambda t, bi=bi: t.matmul(ps[0:64, bi, :], h64_sb[:, :], comb[:, bi, :],
                                       start=True, stop=True))
        p.fence("pe", "se")
        p.wait_dma("se", "sy", stored[bi])
        p.se(lambda s, bi=bi: s.activation(out=w2p_sb[:, bi, :], in_=ps[0:64, bi, :],
                                           func=AF.Identity,
                                           scale=csts[2.0 ** -6.5][0:64, :]))
        evacd[bi] = p.fence("se", "sy", "pe", "gp")
        dst = bass.AP(w2p.tensor, g * 256, [[128 * 256, 64], [1, 256]])
        stored[bi] = p.dma("sy", dst, w2p_sb[:, bi, :])
    p.fence_dma("sy", "sy")
    return b.finish()


def make_bf16_manifest(inputs):
    man = []
    def add(key, arr):
        man.append((key, np.asarray(arr, np.float32)))
    add("L0_ffn_W0a", stream_lhsT(cdT(inputs["L0_ffn_W0a"])))
    add("L0_ffn_W1a", stream_lhsT(cdT(inputs["L0_ffn_W1a"])))
    add("L0_ffn_W0b", stream_lhsT(cdT(inputs["L0_ffn_W0b"])))
    add("L0_ffn_W1b", stream_lhsT(cdT(inputs["L0_ffn_W1b"])))
    add("L1_ffn_W1", stream_lhsT(cdT(inputs["L1_ffn_W1"])))
    add("L1_ffn_W3", stream_lhsT(cdT(inputs["L1_ffn_W3"])))
    add("L0_attn_inT", cdT(inputs["L0_attn_in_proj_w"]))
    add("L0_attn_outT", stream_lhsT(cdT(inputs["L0_attn_out_proj_w"])))
    add("L1_attn_W_inT", cdT(inputs["L1_attn_W_in"]))
    add("L1_attn_WdtT", cdT(inputs["L1_attn_Wdt"]))
    add("L1_attn_WoutT", cdT(inputs["L1_attn_Wout"]))
    return man


def run_prep(inputs, extra_man=()):
    man = make_bf16_manifest(inputs) + list(extra_man)
    flat = [a.ravel() for _, a in man]
    total = int(sum(a.size for a in flat))
    per = -(-total // (NCORES * PREP_TILE)) * PREP_TILE
    blob = np.zeros(NCORES * per, np.float32)
    blob[:total] = np.concatenate(flat)
    w2T = cdT(inputs["L1_ffn_W2"])
    h128m = _hmat(128).astype(BF)
    h64m = _hmat(64).astype(BF)

    nc = build_prep2(per // PREP_TILE)
    in_maps = []
    for c in range(NCORES):
        in_maps.append(dict(
            blob=blob[c * per:(c + 1) * per],
            w2Ts=cd(w2T[:, c * 256:(c + 1) * 256]),
            h128=h128m, h64=h64m,
        ))
    res = run_launch(nc, in_maps)
    blob_o = np.concatenate([r["blob_o"] for r in res.results])
    out = {}
    off = 0
    for key, a in man:
        out[key] = blob_o[off:off + a.size].reshape(a.shape)
        off += a.size
    w2pT = np.concatenate([r["w2p"] for r in res.results], axis=1)
    out["W2P"] = stream_lhsT(w2pT)
    return out


def _hmat(n):
    i = np.arange(n)
    pc = np.bitwise_count((i[:, None] & i[None, :]).astype(np.int64)).astype(np.int64)
    return np.where(pc % 2 == 0, np.float32(1.0), np.float32(-1.0))



def build_ffn1(tok=TOK, tt=512, debug=False):
    KC = D // 128
    MH = 8192 // 128
    MO = D // 128
    NT = tok // tt

    b = Nc()
    xT = b.inp("xT", [D, tok])
    w1 = b.inp("w1", [MH, 128, KC * 128], BF16)
    w3 = b.inp("w3", [MH, 128, KC * 128], BF16)
    w2p = b.inp("w2p", [MO, 128, MH * 128], BF16)
    onesf_d = b.inp("onesf", [128, 1])
    ones_r_d = b.inp("ones_r", [1, 128])
    yT = b.out("yT", [D, tok])

    xTv = xT.rearrange("(c p) t -> p c t", p=128)
    yTv = yT.rearrange("(c p) t -> p c t", p=128)

    x_t = b.sb("x_t", [128, KC, tt])
    sq = b.sb("sq", [128, KC, tt])
    u = b.sb("u", [128, KC, tt], BF16)
    hid = b.sb("hid", [128, MH, tt], BF16)
    rows = b.sb("rows", [1, 4, tt])
    sg = b.sb("sg", [128, tt], BF16)
    res = b.sb("res", [128, 2, tt])
    onesf = b.sb("onesf_sb", [128, 1])
    ones_r = b.sb("ones_r_sb", [1, 128])
    wA1 = b.sb("wA1", [128, 2, KC * 128], BF16)
    wA3 = b.sb("wA3", [128, 2, KC * 128], BF16)
    wB = b.sb("wB", [128, 2, MH * 128], BF16)

    ps_h = [b.ps("ps_h0", [128, tt]), b.ps("ps_h1", [128, tt])]
    ps_o0 = b.ps("ps_o0", [128, tt])
    ps_o1 = b.ps("ps_o1", [128, tt])
    ps_o = [ps_o0, ps_o1]
    ps_b = [b.ps("ps_b0", [128, tt]), b.ps("ps_b1", [128, tt])]
    ps_misc = b.ps("ps_misc", [128, tt])

    if debug:
        dbg_u = b.out("dbg_u", [D, tt], BF16)
        dbg_hid = b.out("dbg_hid", [8192, tt], BF16)
        dbg_hid2 = b.out("dbg_hid2", [8192, tt], BF16)
        dbg_ps = b.out("dbg_ps", [128, tt])
        dbg_uv = dbg_u.rearrange("(c p) t -> p c t", p=128)
        dbg_hv = dbg_hid.rearrange("(c p) t -> p c t", p=128)
        dbg_hv2 = dbg_hid2.rearrange("(c p) t -> p c t", p=128)

    p = b.start()
    csts = make_consts(b, p, [EPS, -0.5])
    p.dma("sy", onesf[:, :], onesf_d)
    p.dma("sy", ones_r[:, :], ones_r_d)
    p.fence_dma("sy", "pe", "ve", "se")

    h_evac_last = 0
    o_evac_ve = 0
    wA_use = [0, 0]
    wB_use = [0, 0]
    res_store = [0, 0]

    for it in range(NT):
        t0 = it * tt
        if it > 0:
            p.fence("ve", "sy")
        p.dma("sy", x_t[:, :, :], xTv[:, :, t0:t0 + tt])
        p.fence_dma("sy", "se", "pe", "ve")
        if it > 0:
            p.fence("ve", "pe")

        emit_ln_rows(p, x_t, sq, KC, tt, onesf, ps_misc, rows, csts)
        p.fence("se", "pe", "ve")

        p.pe(lambda t: t.matmul(ps_b[0][:, :], ones_r[:, :], rows[0:1, 0, :], start=True, stop=True))
        p.pe(lambda t: t.matmul(ps_b[1][:, :], ones_r[:, :], rows[0:1, 1, :], start=True, stop=True))
        p.fence("pe", "ve")

        mean_b = free_bcast(ps_b[0][:, 0:1], KC, tt)
        rstd_b = free_bcast(ps_b[1][:, 0:1], KC, tt)
        p.ve(lambda v: v.tensor_sub(sq[:, :, :], x_t[:, :, :], mean_b))
        p.ve(lambda v: v.tensor_mul(u[:, :, :], sq[:, :, :], rstd_b))
        p.fence("ve", "pe", "gp")

        if debug and it == 0:
            p.dma("sy", dbg_uv[:, :, :], u[:, :, :])

        for m in range(MH):
            buf = m % 2
            p.wait("gp", "pe", wA_use[buf])
            p.dma("gp", wA1[:, buf, :], w1[m])
            ldv = p.dma("gp", wA3[:, buf, :], w3[m])
            p.wait_dma("pe", "gp", ldv)
            p.wait("pe", "ve", h_evac_last)
            for k in range(KC):
                p.pe(lambda t, k=k, buf=buf: t.matmul(
                    ps_h[0][:, :], wA1[:, buf, k * 128:(k + 1) * 128], u[:, k, :],
                    start=(k == 0), stop=(k == KC - 1)))
            for k in range(KC):
                p.pe(lambda t, k=k, buf=buf: t.matmul(
                    ps_h[1][:, :], wA3[:, buf, k * 128:(k + 1) * 128], u[:, k, :],
                    start=(k == 0), stop=(k == KC - 1)))
            wA_use[buf] = p.fence("pe", "se")
            p.wait("se", "ve", h_evac_last)
            p.se(lambda s: s.activation(out=sg[:, :], in_=ps_h[0][:, :], func=AF.Silu))
            p.fence("se", "ve")
            p.ve(lambda v, m=m: v.tensor_mul(hid[:, m, :], sg[:, :], ps_h[1][:, :]))
            h_evac_last = p.fence("ve", "pe")

        if debug and it == 0:
            p.fence("ve", "sy")
            p.dma("sy", dbg_hv[:, :, :], hid[:, :, :])

        for mo in range(MO):
            buf = mo % 2
            p.wait("gp", "pe", wB_use[buf])
            ldv = p.dma("gp", wB[:, buf, :], w2p[mo])
            p.wait_dma("pe", "gp", ldv)
            p.wait("pe", "ve", o_evac_ve)
            for k in range(MH):
                p.pe(lambda t, k=k, buf=buf, mo=mo: t.matmul(
                    ps_o[mo % 2][:, :], wB[:, buf, k * 128:(k + 1) * 128], hid[:, k, :],
                    start=(k == 0), stop=(k == MH - 1)))
            wB_use[buf] = p.fence("pe", "ve")
            if debug and it == 0 and mo == 0:
                p.ve(lambda v: v.tensor_copy(sq[:, 0, :], ps_o[0][:, :]))
                p.fence("ve", "sy")
                p.dma("sy", dbg_ps, sq[:, 0, :])
            p.wait_dma("ve", "sy", res_store[buf])
            p.ve(lambda v, mo=mo, buf=buf: v.scalar_tensor_tensor(
                out=res[:, buf, :], in0=ps_o[mo % 2][:, :], scalar=0.5,
                in1=x_t[:, mo, :], op0=ALU.mult, op1=ALU.add))
            o_evac_ve = p.fence("ve", "sy")
            res_store[buf] = p.dma("sy", yTv[:, mo, t0:t0 + tt], res[:, buf, :])

        if debug and it == 0:
            p.fence("pe", "sy")
            p.dma("sy", dbg_hv2[:, :, :], hid[:, :, :])

    p.fence_dma("sy", "sy")
    return b.finish()


def host_ffn1(inputs, bw):
    return dict(
        w1=bw["L1_ffn_W1"], w3=bw["L1_ffn_W3"], w2p=bw["W2P"],
        onesf=np.ones((128, 1), np.float32),
        ones_r=np.ones((1, 128), np.float32),
    )





def barrier(p, engines=("pe", "ve", "se")):
    vals = {e: p.mark(e) for e in engines}
    for c in engines:
        for e in engines:
            if e != c:
                p.wait(c, e, vals[e])


def build_att0(sl=512, debug=False):
    KC = D // 128
    NSLAB = (B * L) // sl
    NCH = sl // 64

    b = Nc()
    xT = b.inp("xT", [D, B * L])
    wz = b.inp("wz", [4, 128, KC * 128], BF16)
    wx = b.inp("wx", [4, 128, KC * 128], BF16)
    wbc = b.inp("wbc", [2, 128, KC * 128], BF16)
    wdt = b.inp("wdt", [128, KC, 8], BF16)
    wcv = b.inp("wcv", [128, 6, 4])
    cbv = b.inp("cbv", [128, 6])
    alog = b.inp("alog", [8, 1])
    dtb = b.inp("dtb", [8, 1])
    drep = b.inp("drep", [128, 4])
    sel8 = b.inp("sel8", [8, 128])
    cmask8 = b.inp("cmask8", [8, 4])
    maskT_d = b.inp("maskT", [64, 64])
    identb_d = b.inp("identb", [128, 128], BF16)
    identf_d = b.inp("identf", [128, 128])
    onesf_d = b.inp("onesf", [128, 1])
    ones_r_d = b.inp("ones_r", [1, 128])
    ygT = b.out("ygT", [512, B * L])
    ssq = b.out("ssq", [1, B * L])
    cumd = b.scratch("cumd", [NCH * 8 * 64])

    xTv = xT.rearrange("(c p) t -> p c t", p=128)
    ygv = ygT.rearrange("(c p) t -> p c t", p=128)

    x_t = b.sb("x_t", [128, KC, sl])
    sq = b.sb("sq", [128, KC, sl])
    u = b.sb("u", [128, KC, sl], BF16)
    rows = b.sb("rows", [1, 4, sl])
    z_sb = b.sb("z_sb", [128, 4, sl], BF16)
    xsp = b.sb("xsp", [128, 6, sl + 3], BF16)
    cv = b.sb("cv", [128, sl])
    xc = b.sb("xc", [128, 6, sl], BF16)
    dtr = b.sb("dtr", [8, 4, sl])
    cum = b.sb("cum", [8, sl])
    cumflat = b.sb("cumflat", [1, NCH * 8 * 64])
    z64 = b.sb("z64", [8, 64])
    XT_sb = b.sb("XT_sb", [64, 512], BF16)
    BT_sb = b.sb("BT_sb", [64, 128], BF16)
    mt_sb = b.sb("mt_sb", [64, 512], BF16)
    dts = b.sb("dts", [64, 512])
    xft = b.sb("xft", [64, 512], BF16)
    ct_sb = b.sb("ct_sb", [128, 512], BF16)
    cbm = b.sb("cbm", [64, 64])
    cT_sb = b.sb("cT_sb", [64, 8])
    dT_sb = b.sb("dT_sb", [64, 8])
    ft_sb = b.sb("ft_sb", [64, 8])
    s_sb = b.sb("s_sb", [128, 4, 128])
    s0t_sb = b.sb("s0t_sb", [128, 4, 128], BF16)
    zs_sb = b.sb("zs_sb", [128, 4, sl], BF16)
    a8 = b.sb("a8", [8, 3])
    ssq_row = b.sb("ssq_row", [1, sl])
    win = b.sb("win", [128, 10, KC * 128], BF16)
    wdt_sb = b.sb("wdt_sb", [128, KC, 8], BF16)
    wc_sb = b.sb("wc_sb", [128, 6, 4])
    cb_sb = b.sb("cb_sb", [128, 6])
    alog_sb = b.sb("alog_sb", [8, 1])
    dtb_sb = b.sb("dtb_sb", [8, 1])
    drep_sb = b.sb("drep_sb", [128, 4])
    sel8_sb = b.sb("sel8_sb", [8, 128])
    cm8_sb = b.sb("cm8_sb", [8, 4])
    maskT_sb = b.sb("maskT_sb", [64, 64])
    identb_sb = b.sb("identb_sb", [128, 128], BF16)
    identf_sb = b.sb("identf_sb", [128, 128])
    onesf = b.sb("onesf_sb", [128, 1])
    ones_r = b.sb("ones_r_sb", [1, 128])
    small = b.sb("small", [8, 8])

    if debug:
        dbg = {}
        for nm, shp, dt in [("u", [D, 512], BF16), ("z", [512, 512], BF16),
                            ("xc", [768, 512], BF16), ("dt", [8, 512], F32),
                            ("cum", [8, 512], F32), ("XT", [64, 512], BF16),
                            ("BT", [64, 128], BF16), ("mt", [64, 512], BF16),
                            ("ct", [128, 512], BF16), ("expd", [64, 512], F32),
                            ("s1", [128, 4 * 128], F32), ("y1", [512, 64], F32),
                            ("ft", [64, 8], F32), ("sdob", [128, 512], F32)]:
            dbg[nm] = b.out("dbg_" + nm, shp, dt)

    psA = b.ps("psA", [128, sl])
    psB = b.ps("psB", [128, sl])
    psC = b.ps("psC", [128, sl])
    psD = b.ps("psD", [128, 4, 128])
    psE = b.ps("psE", [128, 4, 128])
    psF = b.ps("psF", [128, 64])
    psG = b.ps("psG", [128, 64])
    psH = b.ps("psH", [64, 5, 128], BF16)
    psDf = psD[:, :, :].rearrange("p a b -> p (a b)")
    psEf = psE[:, :, :].rearrange("p a b -> p (a b)")

    p = b.start()
    csts = make_consts(b, p, [EPS, -0.5, 1.0])

    p.dma("sy", wdt_sb[:, :, :], wdt)
    p.dma("sy", wc_sb[:, :, :], wcv)
    p.dma("sy", cb_sb[:, :], cbv)
    p.dma("sy", alog_sb[:, :], alog)
    p.dma("sy", dtb_sb[:, :], dtb)
    p.dma("sy", drep_sb[:, :], drep)
    p.dma("sy", sel8_sb[:, :], sel8)
    p.dma("sy", cm8_sb[:, :], cmask8)
    p.dma("sy", maskT_sb[:, :], maskT_d)
    p.dma("sy", identb_sb[:, :], identb_d)
    p.dma("sy", identf_sb[:, :], identf_d)
    p.dma("sy", onesf[:, :], onesf_d)
    p.dma("sy", ones_r[:, :], ones_r_d)
    for i in range(4):
        p.dma("sy", win[:, i, :], wz[i])
        p.dma("sy", win[:, 4 + i, :], wx[i])
    for i in range(2):
        p.dma("sy", win[:, 8 + i, :], wbc[i])
    p.fence_dma("sy", "pe", "ve", "se")
    p.ve(lambda v: v.memset(z64[:, :], 0.0))
    p.ve(lambda v: v.memset(s_sb[:, :, :], 0.0))
    p.ve(lambda v: v.memset(xsp[:, :, :], 0.0))
    p.fence("ve", "pe", "se")
    p.se(lambda s: s.activation(out=a8[:, 0:1], in_=alog_sb[:, :], func=AF.Exp))
    p.fence("se", "ve")
    p.ve(lambda v: v.tensor_scalar_mul(a8[:, 0:1], a8[:, 0:1], -1.0))
    barrier(p)

    for slab in range(NSLAB):
        bidx, srel = slab // (NSLAB // B), slab % (NSLAB // B)
        t0 = bidx * L + srel * sl

        p.fence("ve", "sy")
        p.dma("sy", x_t[:, :, :], xTv[:, :, t0:t0 + sl])
        p.fence_dma("sy", "se", "pe", "ve")
        if slab > 0:
            p.fence("ve", "pe")
        emit_ln_rows(p, x_t, sq, KC, sl, onesf, psC, rows, csts)
        p.fence("se", "pe", "ve")
        p.pe(lambda t: t.matmul(psA[:, :], ones_r[:, :], rows[0:1, 0, :], start=True, stop=True))
        p.pe(lambda t: t.matmul(psB[:, :], ones_r[:, :], rows[0:1, 1, :], start=True, stop=True))
        p.fence("pe", "ve")
        mean_b = free_bcast(psA[:, 0:1], KC, sl)
        rstd_b = free_bcast(psB[:, 0:1], KC, sl)
        p.ve(lambda v: v.tensor_sub(sq[:, :, :], x_t[:, :, :], mean_b))
        p.ve(lambda v: v.tensor_mul(u[:, :, :], sq[:, :, :], rstd_b))
        barrier(p)

        if debug and slab == 0:
            p.fence("ve", "sy")
            p.dma("sy", dbg["u"].rearrange("(c p) t -> p c t", p=128), u[:, :, :])

        if srel == 0:
            p.ve(lambda v: v.memset(xsp[:, :, 0:3], 0.0))
            if slab > 0:
                p.ve(lambda v: v.memset(s_sb[:, :, :], 0.0))
        else:
            p.ve(lambda v: v.tensor_copy(xsp[:, :, 0:3], xsp[:, :, sl:sl + 3]))
        p.fence("ve", "se", "pe")

        pp = [psDf, psEf]
        for m in range(10):
            ps_ = pp[m % 2]
            for k in range(KC):
                p.pe(lambda t, k=k, m=m, ps_=ps_: t.matmul(
                    ps_[:, :], win[:, m, k * 128:(k + 1) * 128], u[:, k, :],
                    start=(k == 0), stop=(k == KC - 1)))
            p.fence("pe", "se")
            if m < 4:
                p.se(lambda s, m=m, ps_=ps_: s.copy(z_sb[:, m, :], ps_[:, :]))
            else:
                p.se(lambda s, m=m, ps_=ps_: s.copy(xsp[:, m - 4, 3:3 + sl], ps_[:, :]))
            p.fence("se", "pe")
        for k in range(KC):
            p.pe(lambda t, k=k: t.matmul(psC[0:8, 0:sl], wdt_sb[:, k, :], u[:, k, :],
                                         start=(k == 0), stop=(k == KC - 1)))
        barrier(p)

        for ti in range(6):
            p.ve(lambda v, ti=ti: v.tensor_scalar(
                out=cv[:, :], in0=xsp[:, ti, 0:sl], scalar1=wc_sb[:, ti, 0:1],
                scalar2=None, op0=ALU.mult))
            for j in range(1, 4):
                p.ve(lambda v, ti=ti, j=j: v.scalar_tensor_tensor(
                    out=cv[:, :], in0=xsp[:, ti, j:j + sl], scalar=wc_sb[:, ti, j:j + 1],
                    in1=cv[:, :], op0=ALU.mult, op1=ALU.add))
            p.fence("ve", "se")
            p.se(lambda s, ti=ti: s.activation(out=xc[:, ti, :], in_=cv[:, :],
                                               func=AF.Silu, bias=cb_sb[:, ti:ti + 1]))
            p.fence("se", "ve")
        barrier(p)

        if debug and slab == 0:
            p.fence("se", "sy")
            p.fence("ve", "sy")
            p.dma("sy", dbg["z"].rearrange("(c p) t -> p c t", p=128), z_sb[:, :, :])
            p.dma("sy", dbg["xc"].rearrange("(c p) t -> p c t", p=128), xc[:, :, :].rearrange("p c t -> p c t")[:, 0:6, :])

        p.ve(lambda v: v.tensor_scalar(out=dtr[:, 0, :], in0=psC[0:8, 0:sl],
                                       scalar1=dtb_sb[:, :], scalar2=None, op0=ALU.add))
        p.ve(lambda v: v.tensor_scalar(out=dtr[:, 1, :], in0=dtr[:, 0, :],
                                       scalar1=0.0, scalar2=None, op0=ALU.max))
        p.fence("ve", "se", "pe")
        p.se(lambda s: s.activation(out=dtr[:, 2, :], in_=dtr[:, 0, :], func=AF.Abs))
        p.se(lambda s: s.activation(out=dtr[:, 3, :], in_=dtr[:, 2, :], func=AF.Exp, scale=-1.0))
        p.se(lambda s: s.activation(out=dtr[:, 2, :], in_=dtr[:, 3, :], func=AF.Ln,
                                    bias=csts[1.0][0:8, :]))
        p.fence("se", "ve")
        p.ve(lambda v: v.tensor_add(dtr[:, 3, :], dtr[:, 1, :], dtr[:, 2, :]))
        p.ve(lambda v: v.tensor_scalar(out=dtr[:, 0, :], in0=dtr[:, 3, :],
                                       scalar1=a8[:, 0:1], scalar2=None, op0=ALU.mult))
        p.fence("ve", "ve")
        for cc in range(NCH):
            p.ve(lambda v, cc=cc: v.tensor_tensor_scan(
                cum[:, cc * 64:(cc + 1) * 64], dtr[:, 0, cc * 64:(cc + 1) * 64],
                z64[:, :], 0.0, ALU.add, ALU.add))
        p.fence("ve", "sy")
        p.dma("sy", cumd[:], cum[:, :])
        p.fence_dma("sy", "sy")
        p.dma("sy", cumflat[0:1, :], cumd[None, :])
        p.fence_dma("sy", "se", "pe", "ve")
        barrier(p)

        if debug and slab == 0:
            p.fence("ve", "sy")
            p.dma("sy", dbg["dt"], dtr[:, 3, :])
            p.dma("sy", dbg["cum"], cum[:, :])

        cfl = cumflat[0:1, :].rearrange("p (h t) -> p h t", t=sl)

        for cc in range(NCH):
            c0 = cc * 64
            for t in range(4):
                p.pe(lambda te, t=t, c0=c0: te.transpose(psH[:, t, :], xc[:, t, c0:c0 + 64], identb_sb[:, :]))
            p.pe(lambda te, c0=c0: te.transpose(psH[:, 4, :], xc[:, 4, c0:c0 + 64], identb_sb[:, :]))
            p.fence("pe", "se")
            p.se(lambda s: s.copy(XT_sb[:, :], psH[:, 0:4, :]))
            p.se(lambda s: s.copy(BT_sb[:, :], psH[:, 4, :]))
            p.fence("se", "pe", "ve")
            p.pe(lambda te, c0=c0: te.transpose(psC[0:64, 72:80], cum[:, c0:c0 + 64], identf_sb[0:8, 0:8]))
            p.pe(lambda te, c0=c0: te.transpose(psC[0:64, 80:88], dtr[:, 3, c0:c0 + 64], identf_sb[0:8, 0:8]))
            p.pe(lambda te, cc=cc: te.matmul(psC[0:64, 88:96], ones_r[0:1, 0:64],
                                             cfl[:, :, cc * 64 + 63], start=True, stop=True))
            p.fence("pe", "se", "ve")
            p.se(lambda s, cc=cc: s.activation(out=a8[:, 1:2], in_=cum[:, cc * 64 + 63:cc * 64 + 64], func=AF.Exp))
            p.fence("se", "ve")
            p.ve(lambda v: v.tensor_scalar(out=small[0:8, 0:4], in0=cm8_sb[:, :],
                                           scalar1=a8[:, 1:2], scalar2=None, op0=ALU.mult))
            p.ve(lambda v: v.tensor_copy(cT_sb[:, :], psC[0:64, 72:80]))
            p.ve(lambda v: v.tensor_copy(dT_sb[:, :], psC[0:64, 80:88]))
            p.fence("ve", "pe")
            p.pe(lambda te: te.matmul(psC[0:128, 96:100], sel8_sb[:, :], small[0:8, 0:4], start=True, stop=True))
            p.pe(lambda te, cc=cc: te.matmul(psA[0:64, 0:512], ones_r[0:1, 0:64],
                                             cfl[:, :, cc * 64:(cc + 1) * 64], start=True, stop=True))
            p.pe(lambda te, cc=cc: te.matmul(psB[:, 0:512], ones_r[:, :],
                                             cfl[:, :, cc * 64:(cc + 1) * 64], start=True, stop=True))
            p.pe(lambda te, c0=c0: te.matmul(psC[0:64, 0:64], xc[:, 4, c0:c0 + 64],
                                             xc[:, 5, c0:c0 + 64], start=True, stop=True))
            p.fence("pe", "ve", "se")
            p.se(lambda s: s.activation(out=psB[:, 0:512], in_=psB[:, 0:512], func=AF.Exp))
            p.fence("se", "ve")
            ctT_bc = bass.AP(cT_sb.ap().tensor, cT_sb.ap().offset,
                             [cT_sb.ap().ap[0], [1, 8], [0, 64]])
            p.ve(lambda v: v.scalar_tensor_tensor(out=dts[:, :], in0=ctT_bc, scalar=-1.0,
                                                  in1=psA[0:64, 0:512], op0=ALU.mult, op1=ALU.add))
            p.ve(lambda v: v.tensor_scalar(out=dts[:, :], in0=dts[:, :], scalar1=0.0,
                                           scalar2=None, op0=ALU.min))
            p.ve(lambda v: v.scalar_tensor_tensor(out=ft_sb[:, :], in0=cT_sb[:, :], scalar=-1.0,
                                                  in1=psC[0:64, 88:96], op0=ALU.mult, op1=ALU.add))
            p.ve(lambda v: v.tensor_mul(cbm[:, :], psC[0:64, 0:64], maskT_sb[:, :]))
            p.fence("ve", "se")
            p.se(lambda s: s.activation(out=dts[:, :], in_=dts[:, :], func=AF.Exp))
            p.se(lambda s: s.activation(out=ft_sb[:, :], in_=ft_sb[:, :], func=AF.Exp))
            p.fence("se", "ve")
            p.ve(lambda v: v.tensor_mul(ft_sb[:, :], ft_sb[:, :], dT_sb[:, :]))
            cbm_bc = bass.AP(cbm.ap().tensor, cbm.ap().offset, [cbm.ap().ap[0], [0, 8], [1, 64]])
            p.ve(lambda v: v.tensor_mul(dts[:, :], dts[:, :], cbm_bc))
            dT_bc = bass.AP(dT_sb.ap().tensor, dT_sb.ap().offset, [dT_sb.ap().ap[0], [1, 8], [0, 64]])
            p.ve(lambda v: v.tensor_mul(mt_sb[:, :], dts[:, :], dT_bc))
            ft_bc = bass.AP(ft_sb.ap().tensor, ft_sb.ap().offset, [ft_sb.ap().ap[0], [1, 8], [0, 64]])
            p.ve(lambda v: v.tensor_mul(xft[:, :], XT_sb[:, :], ft_bc))
            xcC_bc = bass.AP(xc.ap().tensor, xc.ap().offset + 5 * sl + c0,
                             [xc.ap().ap[0], [0, 8], [1, 64]])
            p.ve(lambda v: v.tensor_mul(ct_sb[:, :], xcC_bc, psB[:, 0:512]))
            barrier(p)
            for h in range(8):
                t = h // 2
                p.pe(lambda te, h=h, t=t: te.matmul(
                    psD[(h % 2) * 64:(h % 2) * 64 + 64, t, :],
                    xft[:, t * 128 + (h % 2) * 64: t * 128 + (h % 2) * 64 + 64],
                    BT_sb[:, :], start=True, stop=True))
            p.fence("pe", "ve")
            for t in range(4):
                p.ve(lambda v, t=t: v.scalar_tensor_tensor(
                    out=s_sb[:, t, :], in0=s_sb[:, t, :], scalar=psC[:, 96 + t:97 + t],
                    in1=psD[:, t, :], op0=ALU.mult, op1=ALU.add))
            p.fence("ve", "pe")
            for t in range(4):
                p.pe(lambda te, t=t: te.transpose(psE[:, t, :], s_sb[:, t, :], identf_sb[:, :]))
            p.fence("pe", "se")
            p.se(lambda s: s.copy(s0t_sb[:, :, :], psE[:, :, :]))
            p.fence("se", "pe")
            for t in range(4):
                bank = psF if t % 2 == 0 else psG
                for hh in range(2):
                    h = 2 * t + hh
                    p.pe(lambda te, h=h, t=t, hh=hh, bank=bank: te.matmul(
                        bank[hh * 64:hh * 64 + 64, :],
                        XT_sb[:, t * 128 + hh * 64: t * 128 + hh * 64 + 64],
                        mt_sb[:, h * 64:(h + 1) * 64], start=True, stop=False))
                    p.pe(lambda te, h=h, t=t, hh=hh, bank=bank: te.matmul(
                        bank[hh * 64:hh * 64 + 64, :],
                        s0t_sb[:, t, hh * 64:hh * 64 + 64],
                        ct_sb[:, h * 64:(h + 1) * 64], start=False, stop=True))
                p.fence("pe", "ve")
                p.ve(lambda v, t=t, c0=c0, bank=bank: v.scalar_tensor_tensor(
                    out=sq[:, t, c0:c0 + 64], in0=xc[:, t, c0:c0 + 64],
                    scalar=drep_sb[:, t:t + 1], in1=bank[:, :], op0=ALU.mult, op1=ALU.add))
                p.fence("ve", "pe")
            if debug and slab == 0 and cc == 0:
                p.fence("ve", "sy")
                p.fence("se", "sy")
                p.dma("sy", dbg["XT"], XT_sb[:, :])
                p.dma("sy", dbg["BT"], BT_sb[:, :])
                p.dma("sy", dbg["mt"], mt_sb[:, :])
                p.dma("sy", dbg["ct"], ct_sb[:, :])
                p.dma("sy", dbg["expd"], dts[:, :])
                p.dma("sy", dbg["ft"], ft_sb[:, :])
                p.ve(lambda v: v.tensor_copy(cv[:, :], psB[:, 0:512]))
                p.fence("ve", "sy")
                p.dma("sy", dbg["sdob"], cv[:, 0:512])
                p.dma("sy", dbg["s1"], s_sb[:, :, :].rearrange("p a b -> p (a b)"))
                p.dma("sy", dbg["y1"].rearrange("(c p) t -> p c t", p=128), sq[:, 0:4, 0:64])
                p.fence_dma("sy", "pe", "ve", "se")
            barrier(p)

        p.se(lambda s: s.activation(out=zs_sb[:, :, :], in_=z_sb[:, :, :], func=AF.Silu))
        p.fence("se", "ve")
        p.ve(lambda v: v.tensor_mul(sq[:, 4:8, :], sq[:, 0:4, :], zs_sb[:, :, :]))
        p.fence("ve", "se")
        p.se(lambda s: s.activation(out=sq[:, 8:12, :], in_=sq[:, 4:8, :], func=AF.Square))
        p.fence("se", "pe")
        for k in range(4):
            p.pe(lambda t, k=k: t.matmul(psC[0:1, 0:sl], onesf[:, :], sq[:, 8 + k, :],
                                         start=(k == 0), stop=(k == 3)))
        p.fence("pe", "se")
        p.se(lambda s: s.copy(ssq_row[:, :], psC[0:1, 0:sl]))
        p.fence("se", "sy")
        p.dma("sy", ygv[:, :, t0:t0 + sl], sq[:, 4:8, :])
        p.dma("sy", ssq[0:1, t0:t0 + sl], ssq_row[:, :])
        barrier(p)

    p.fence_dma("sy", "sy")
    return b.finish()


def host_att0(inputs, bw, core):
    inT = np.asarray(bw["L0_attn_inT"])
    cwf = np.asarray(inputs["L0_attn_conv_w"]).reshape(4352, 4)
    cbf = np.asarray(inputs["L0_attn_conv_b"])
    c = core
    zc = inT[:, c * 512:(c + 1) * 512]
    xcp = inT[:, 4096 + c * 512:4096 + (c + 1) * 512]
    bcc = inT[:, 8192:8448]
    dtc = inT[:, 8448 + 8 * c:8456 + 8 * c]
    wcl = np.concatenate([cwf[c * 512:(c + 1) * 512], cwf[4096:4352]], axis=0)
    cbl = np.concatenate([cbf[c * 512:(c + 1) * 512], cbf[4096:4352]])
    hs = slice(8 * c, 8 * c + 8)
    i64 = np.arange(64)
    sel = (np.arange(8)[:, None] % 2 == (np.arange(128)[None, :] // 64)).astype(np.float32)
    cm8 = (np.arange(8)[:, None] // 2 == np.arange(4)[None, :]).astype(np.float32)
    maskT = (i64[:, None] <= i64[None, :]).astype(np.float32)
    return dict(
        wz=stream_lhsT(zc), wx=stream_lhsT(xcp), wbc=stream_lhsT(bcc),
        wdt=cd(dtc.reshape(16, 128, 8).transpose(1, 0, 2)),
        wcv=cd(wcl.reshape(6, 128, 4).transpose(1, 0, 2).astype(np.float32)),
        cbv=tile128(cbl.astype(np.float32)),
        alog=cd(np.asarray(inputs["L0_attn_A_log"])[hs, None].astype(np.float32)),
        dtb=cd(np.asarray(inputs["L0_attn_dt_bias"])[hs, None].astype(np.float32)),
        drep=tile128(np.repeat(np.asarray(inputs["L0_attn_D"])[hs], 64).astype(np.float32)),
        sel8=sel, cmask8=cm8, maskT=maskT,
        identb=np.eye(128, dtype=BF), identf=np.eye(128, dtype=np.float32),
        onesf=np.ones((128, 1), np.float32), ones_r=np.ones((1, 128), np.float32),
    )



def build_att0fin(tok=TOK, tt=512):
    MI = 4096 // 128
    MO = D // 128
    NT = tok // tt

    b = Nc()
    xT = b.inp("xT", [D, tok])
    ygT = b.inp("ygT", [4096, tok])
    ssq8 = b.inp("ssq8", [8, tok])
    nw = b.inp("nw", [128, MI])
    wo = b.inp("wo", [MO, 128, MI * 128], BF16)
    onesf_d = b.inp("onesf", [128, 1])
    ones_r_d = b.inp("ones_r", [1, 128])
    yT = b.out("yT", [D, tok])

    xTv = xT.rearrange("(c p) t -> p c t", p=128)
    ygvv = ygT.rearrange("(c p) t -> p c t", p=128)
    yTv = yT.rearrange("(c p) t -> p c t", p=128)

    x_t = b.sb("x_t", [128, MO, tt])
    yg = b.sb("yg", [128, MI, tt])
    yn = b.sb("yn", [128, MI, tt], BF16)
    ss = b.sb("ss", [8, tt])
    rrow = b.sb("rrow", [1, 2, tt])
    res = b.sb("res", [128, 2, tt])
    nw_sb = b.sb("nw_sb", [128, MI])
    onesf = b.sb("onesf_sb", [128, 1])
    ones_r = b.sb("ones_r_sb", [1, 128])
    wB = b.sb("wB", [128, 2, MI * 128], BF16)
    psR = b.ps("psR", [128, tt])
    psO = [b.ps("psO0", [128, tt]), b.ps("psO1", [128, tt])]

    p = b.start()
    csts = make_consts(b, p, [EPS, -0.5, 1.0 / 4096.0])
    p.dma("sy", nw_sb[:, :], nw)
    p.dma("sy", onesf[:, :], onesf_d)
    p.dma("sy", ones_r[:, :], ones_r_d)
    p.fence_dma("sy", "pe", "ve", "se")

    wB_use = [0, 0]
    o_evac = 0
    res_store = [0, 0]
    for it in range(NT):
        t0 = it * tt
        p.fence("ve", "sy")
        p.dma("sy", x_t[:, :, :], xTv[:, :, t0:t0 + tt])
        p.dma("sy", yg[:, :, :], ygvv[:, :, t0:t0 + tt])
        p.dma("sy", ss[:, :], ssq8[:, t0:t0 + tt])
        p.fence_dma("sy", "pe", "ve", "se")
        if it > 0:
            p.fence("ve", "pe")
            p.fence("se", "pe")
        p.pe(lambda t: t.matmul(psR[0:1, 0:tt], onesf[0:8, :], ss[:, :], start=True, stop=True))
        p.fence("pe", "se")
        p.se(lambda s: s.activation(out=rrow[0:1, 0, :], in_=psR[0:1, 0:tt], func=AF.Ln,
                                    scale=csts[1.0 / 4096.0][0:1, :], bias=csts[EPS][0:1, :]))
        p.se(lambda s: s.activation(out=rrow[0:1, 1, :], in_=rrow[0:1, 0, :], func=AF.Exp,
                                    scale=csts[-0.5][0:1, :]))
        p.fence("se", "pe")
        p.pe(lambda t: t.matmul(psR[:, 0:tt], ones_r[:, :], rrow[0:1, 1, :], start=True, stop=True))
        p.fence("pe", "ve")
        rb = free_bcast(psR[:, 0:1], 1, tt)
        for m in range(MI):
            p.ve(lambda v, m=m: v.scalar_tensor_tensor(
                out=yn[:, m, :], in0=yg[:, m, :], scalar=nw_sb[:, m:m + 1],
                in1=psR[:, 0:tt], op0=ALU.mult, op1=ALU.mult))
        p.fence("ve", "pe", "gp")
        for mo in range(MO):
            buf = mo % 2
            p.wait("gp", "pe", wB_use[buf])
            ldv = p.dma("gp", wB[:, buf, :], wo[mo])
            p.wait_dma("pe", "gp", ldv)
            p.wait("pe", "ve", o_evac)
            for k in range(MI):
                p.pe(lambda t, k=k, buf=buf, mo=mo: t.matmul(
                    psO[mo % 2][:, :], wB[:, buf, k * 128:(k + 1) * 128], yn[:, k, :],
                    start=(k == 0), stop=(k == MI - 1)))
            wB_use[buf] = p.fence("pe", "ve")
            p.wait_dma("ve", "sy", res_store[buf])
            p.ve(lambda v, mo=mo, buf=buf: v.tensor_add(
                res[:, buf, :], psO[mo % 2][:, :], x_t[:, mo, :]))
            o_evac = p.fence("ve", "sy")
            res_store[buf] = p.dma("sy", yTv[:, mo, t0:t0 + tt], res[:, buf, :])
    p.fence_dma("sy", "sy")
    return b.finish()



def build_att1a(sl=512):
    KC = D // 128
    NSLAB = (B * L) // sl

    b = Nc()
    xT = b.inp("xT", [D, B * L])
    wh = b.inp("wh", [4, 128, KC * 128], BF16)
    wg = b.inp("wg", [4, 128, KC * 128], BF16)
    wxs = b.inp("wxs", [128, 4, 160])
    wcv = b.inp("wcv", [128, 4, 4])
    cbv = b.inp("cbv", [128, 4])
    onesf_d = b.inp("onesf", [128, 1])
    ones_r_d = b.inp("ones_r", [1, 128])
    hO = b.out("hO", [512, B * L], BF16)
    gO = b.out("gO", [512, B * L], BF16)
    pO = b.out("pO", [160, B * L])

    xTv = xT.rearrange("(c p) t -> p c t", p=128)
    hOv = hO.rearrange("(c p) t -> p c t", p=128)
    gOv = gO.rearrange("(c p) t -> p c t", p=128)

    x_t = b.sb("x_t", [128, KC, sl])
    sq = b.sb("sq", [128, KC, sl])
    u = b.sb("u", [128, KC, sl], BF16)
    rows = b.sb("rows", [1, 4, sl])
    hsp = b.sb("hsp", [128, 4, sl + 3], BF16)
    g_sb = b.sb("g_sb", [128, 4, sl], BF16)
    cv = b.sb("cv", [128, sl])
    hc = b.sb("hc", [128, 4, sl], BF16)
    po1 = b.sb("po1", [128, sl])
    po2 = b.sb("po2", [32, sl])
    win_h = b.sb("win_h", [128, 4, KC * 128], BF16)
    win_g = b.sb("win_g", [128, 4, KC * 128], BF16)
    wx_sb = b.sb("wx_sb", [128, 4, 160], BF16)
    wc_sb = b.sb("wc_sb", [128, 4, 4])
    cb_sb = b.sb("cb_sb", [128, 4])
    onesf = b.sb("onesf_sb", [128, 1])
    ones_r = b.sb("ones_r_sb", [1, 128])

    psA = b.ps("psA", [128, sl])
    psB = b.ps("psB", [128, sl])
    psC = b.ps("psC", [128, sl])
    psD = b.ps("psD", [128, sl])
    psE = b.ps("psE", [128, sl])

    p = b.start()
    csts = make_consts(b, p, [EPS, -0.5])
    for i in range(4):
        p.dma("sy", win_h[:, i, :], wh[i])
        p.dma("sy", win_g[:, i, :], wg[i])
    p.dma("gp", wx_sb[:, :, :], wxs)
    p.dma("sy", wc_sb[:, :, :], wcv)
    p.dma("sy", cb_sb[:, :], cbv)
    p.dma("sy", onesf[:, :], onesf_d)
    p.dma("sy", ones_r[:, :], ones_r_d)
    p.fence_dma("sy", "pe", "ve", "se")
    p.fence_dma("gp", "pe", "ve", "se")
    p.ve(lambda v: v.memset(hsp[:, :, :], 0.0))
    barrier(p)

    pp = [psD, psE]
    for slab in range(NSLAB):
        bidx, srel = slab // (NSLAB // B), slab % (NSLAB // B)
        t0 = bidx * L + srel * sl

        p.fence("ve", "sy")
        p.fence("se", "sy")
        p.dma("sy", x_t[:, :, :], xTv[:, :, t0:t0 + sl])
        p.fence_dma("sy", "se", "pe", "ve")
        if slab > 0:
            p.fence("ve", "pe")
        emit_ln_rows(p, x_t, sq, KC, sl, onesf, psC, rows, csts)
        p.fence("se", "pe", "ve")
        p.pe(lambda t: t.matmul(psA[:, :], ones_r[:, :], rows[0:1, 0, :], start=True, stop=True))
        p.pe(lambda t: t.matmul(psB[:, :], ones_r[:, :], rows[0:1, 1, :], start=True, stop=True))
        p.fence("pe", "ve")
        p.ve(lambda v: v.tensor_sub(sq[:, :, :], x_t[:, :, :], free_bcast(psA[:, 0:1], KC, sl)))
        p.ve(lambda v: v.tensor_mul(u[:, :, :], sq[:, :, :], free_bcast(psB[:, 0:1], KC, sl)))
        barrier(p)

        if srel == 0:
            p.ve(lambda v: v.memset(hsp[:, :, 0:3], 0.0))
        else:
            p.ve(lambda v: v.tensor_copy(hsp[:, :, 0:3], hsp[:, :, sl:sl + 3]))
        p.fence("ve", "se", "pe")

        for m in range(8):
            ps_ = pp[m % 2]
            wsrc = win_h if m < 4 else win_g
            mi = m % 4
            for k in range(KC):
                p.pe(lambda t, k=k, mi=mi, ps_=ps_, wsrc=wsrc: t.matmul(
                    ps_[:, :], wsrc[:, mi, k * 128:(k + 1) * 128], u[:, k, :],
                    start=(k == 0), stop=(k == KC - 1)))
            p.fence("pe", "se")
            if m < 4:
                p.se(lambda s, mi=mi, ps_=ps_: s.copy(hsp[:, mi, 3:3 + sl], ps_[:, :]))
            else:
                p.se(lambda s, mi=mi, ps_=ps_: s.copy(g_sb[:, mi, :], ps_[:, :]))
            p.fence("se", "pe")
        barrier(p)

        for ti in range(4):
            p.ve(lambda v, ti=ti: v.tensor_scalar(
                out=cv[:, :], in0=hsp[:, ti, 0:sl], scalar1=wc_sb[:, ti, 0:1],
                scalar2=None, op0=ALU.mult))
            for j in range(1, 4):
                p.ve(lambda v, ti=ti, j=j: v.scalar_tensor_tensor(
                    out=cv[:, :], in0=hsp[:, ti, j:j + sl], scalar=wc_sb[:, ti, j:j + 1],
                    in1=cv[:, :], op0=ALU.mult, op1=ALU.add))
            p.fence("ve", "se")
            p.se(lambda s, ti=ti: s.activation(out=hc[:, ti, :], in_=cv[:, :],
                                               func=AF.Silu, bias=cb_sb[:, ti:ti + 1]))
            p.fence("se", "ve", "pe")

        p.wait("pe", "ve", 0)
        for k in range(4):
            p.pe(lambda t, k=k: t.matmul(psA[:, :], wx_sb[:, k, 0:128], hc[:, k, :],
                                         start=(k == 0), stop=(k == 3)))
        for k in range(4):
            p.pe(lambda t, k=k: t.matmul(psB[0:32, 0:sl], wx_sb[:, k, 128:160], hc[:, k, :],
                                         start=(k == 0), stop=(k == 3)))
        p.fence("pe", "ve")
        p.ve(lambda v: v.tensor_copy(po1[:, :], psA[:, :]))
        p.ve(lambda v: v.tensor_copy(po2[:, :], psB[0:32, 0:sl]))
        p.fence("ve", "sy", "pe")
        p.dma("sy", hOv[:, :, t0:t0 + sl], hc[:, :, :])
        p.dma("sy", gOv[:, :, t0:t0 + sl], g_sb[:, :, :])
        p.dma("sy", pO[0:128, t0:t0 + sl], po1[:, :])
        p.dma("sy", pO[128:160, t0:t0 + sl], po2[:, :])
        barrier(p)

    p.fence_dma("sy", "sy")
    return b.finish()


def host_att1a(inputs, bw, core):
    winT = np.asarray(bw["L1_attn_W_inT"])
    c = core
    wh = stream_lhsT(winT[:, c * 512:(c + 1) * 512])
    wg = stream_lhsT(winT[:, 4096 + c * 512:4096 + (c + 1) * 512])
    wxT = np.asarray(inputs["L1_attn_Wx"]).T.astype(np.float32)
    wxs = cd(wxT[c * 512:(c + 1) * 512].reshape(4, 128, 160).transpose(1, 0, 2))
    cw = np.asarray(inputs["L1_attn_conv_w"]).astype(np.float32)[c * 512:(c + 1) * 512]
    cb = np.asarray(inputs["L1_attn_conv_b"]).astype(np.float32)[c * 512:(c + 1) * 512]
    return dict(
        wh=wh, wg=wg, wxs=wxs,
        wcv=cd(cw.reshape(4, 128, 4).transpose(1, 0, 2)),
        cbv=tile128(cb),
        onesf=np.ones((128, 1), np.float32), ones_r=np.ones((1, 128), np.float32),
    )



def build_att1b(sl=512, sc=256, debug=False):
    NSLAB = (B * L) // sl
    NSC = sl // sc

    b = Nc()
    hI = b.inp("hI", [512, B * L], BF16)
    gI = b.inp("gI", [512, B * L], BF16)
    pI = b.inp("pI", [8, 160, B * L])
    wdts = b.inp("wdts", [4, 128, 128], BF16)
    a_in = b.inp("a_in", [128, 4, 16])
    dtb = b.inp("dtb", [128, 4])
    dv_in = b.inp("dv_in", [128, 4])
    lnw_dt = b.inp("lnw_dt", [128, 1])
    lnw_bc = b.inp("lnw_bc", [32, 1])
    bsel = b.inp("bsel", [32, 2])
    bsel2 = b.inp("bsel2", [2, 32])
    wo = b.inp("wo", [16, 128, 512], BF16)
    onesf_d = b.inp("onesf", [128, 1])
    ones_r_d = b.inp("ones_r", [1, 128])
    oO = b.out("oO", [D, B * L])
    bcd = b.scratch("bcd", [32 * 512], BF16)

    hIv = hI.rearrange("(c p) t -> p c t", p=128)
    gIv = gI.rearrange("(c p) t -> p c t", p=128)
    oOv = oO.rearrange("(c p) t -> p c t", p=128)

    h_sb = b.sb("h_sb", [128, 4, sl], BF16)
    g_sb = b.sb("g_sb", [128, 4, sl], BF16)
    pp1 = b.sb("pp1", [128, 8, sl])
    pp2 = b.sb("pp2", [32, 8, sl])
    tsn = b.sb("tsn", [128, sl], BF16)
    bcn = b.sb("bcn", [32, sl], BF16)
    sq1 = b.sb("sq1", [128, sl])
    rrow = b.sb("rrow", [1, 2, sl])
    rrow2 = b.sb("rrow2", [2, 2, sl])
    delta = b.sb("delta", [128, 4, sl])
    dsc = b.sb("dsc", [128, 4, sl])
    brep = b.sb("brep", [128, 16 * sl], BF16)
    crep = b.sb("crep", [128, 16 * sl], BF16)
    dA = b.sb("dA", [128, 16 * sc])
    dBu = b.sb("dBu", [128, 16 * sc])
    s_scan = b.sb("s_scan", [128, 16 * sc])
    du = b.sb("du", [128, sc])
    tr1 = b.sb("tr1", [128, 8 * sc])
    state = b.sb("state", [128, 64])
    y_sb = b.sb("y_sb", [128, 4, sl])
    yn = b.sb("yn", [128, 4, sl], BF16)
    oo = b.sb("oo", [128, 2, sl])
    wdt_sb = b.sb("wdt_sb", [128, 4, 128], BF16)
    a_sb = b.sb("a_sb", [128, 4, 16])
    dtb_sb = b.sb("dtb_sb", [128, 4])
    dv_sb = b.sb("dv_sb", [128, 4])
    lnwd_sb = b.sb("lnwd_sb", [128, 1])
    lnwbc_sb = b.sb("lnwbc_sb", [32, 1])
    bsel_sb = b.sb("bsel_sb", [32, 2])
    bsel2_sb = b.sb("bsel2_sb", [2, 32])
    wo_sb = b.sb("wo_sb", [128, 2, 512], BF16)
    onesf = b.sb("onesf_sb", [128, 1])
    ones_r = b.sb("ones_r_sb", [1, 128])

    psA = b.ps("psA", [128, sl])
    psB = b.ps("psB", [128, sl])
    psC = b.ps("psC", [128, sl])
    psD = b.ps("psD", [128, sl])

    if debug:
        dbg = {}
        for nm, shp, dt in [("tsn", [128, sl], BF16), ("bcn", [32, sl], BF16),
                            ("delta", [512, sl], F32), ("dA", [128, 16 * sc], F32),
                            ("dBu", [128, 16 * sc], F32), ("sscan", [128, 16 * sc], F32),
                            ("y", [512, sl], F32), ("brep", [128, 16 * sl], BF16), ("crep", [128, 16 * sl], BF16), ("sc_", [128, 16 * sc], F32), ("tr", [128, sc], F32)]:
            dbg[nm] = b.out("dbg_" + nm, shp, dt)

    p = b.start()
    csts = make_consts(b, p, [EPS, -0.5, 1.0, 1.0 / 128.0, 1.0 / 16.0])
    for m in range(4):
        p.dma("sy", wdt_sb[:, m, :], wdts[m])
    p.dma("sy", a_sb[:, :, :], a_in)
    p.dma("sy", dtb_sb[:, :], dtb)
    p.dma("sy", dv_sb[:, :], dv_in)
    p.dma("sy", lnwd_sb[:, :], lnw_dt)
    p.dma("sy", lnwbc_sb[:, :], lnw_bc)
    p.dma("sy", bsel_sb[:, :], bsel)
    p.dma("sy", bsel2_sb[:, :], bsel2)
    p.dma("sy", onesf[:, :], onesf_d)
    p.dma("sy", ones_r[:, :], ones_r_d)
    p.fence_dma("sy", "pe", "ve", "se")
    p.ve(lambda v: v.memset(state[:, :], 0.0))
    barrier(p)

    wo_use = [0, 0]
    oo_store = [0, 0]
    dt_evac = [0, 0]
    wo_evac = [0, 0]
    for slab in range(NSLAB):
        bidx, srel = slab // (NSLAB // B), slab % (NSLAB // B)
        t0 = bidx * L + srel * sl
        if srel == 0 and slab > 0:
            p.ve(lambda v: v.memset(state[:, :], 0.0))
            p.fence("ve", "ve")

        p.fence("ve", "sy")
        p.fence("se", "sy")
        p.dma("sy", h_sb[:, :, :], hIv[:, :, t0:t0 + sl])
        p.dma("sy", g_sb[:, :, :], gIv[:, :, t0:t0 + sl])
        for j in range(8):
            p.dma("sy", pp1[:, j, :], pI[j, 0:128, t0:t0 + sl])
            p.dma("sy", pp2[:, j, :], pI[j, 128:160, t0:t0 + sl])
        p.fence_dma("sy", "ve", "se", "pe")

        for st_, n_ in ((pp1, 128), (pp2, 32)):
            for half in (4, 2, 1):
                p.ve(lambda v, st_=st_, n_=n_, half=half: v.tensor_add(
                    st_[0:n_, 0:half, :], st_[0:n_, 0:half, :], st_[0:n_, half:2 * half, :]))
                p.mark("ve")
        p.fence("ve", "se", "pe")

        p.se(lambda s: s.activation(out=sq1[:, :], in_=pp1[:, 0, :], func=AF.Square))
        p.fence("se", "pe")
        p.pe(lambda t: t.matmul(psA[0:1, 0:sl], onesf[:, :], sq1[:, :], start=True, stop=True))
        p.fence("pe", "se")
        p.se(lambda s: s.activation(out=rrow[0:1, 0, :], in_=psA[0:1, 0:sl], func=AF.Ln,
                                    scale=csts[1.0 / 128.0][0:1, :], bias=csts[EPS][0:1, :]))
        p.se(lambda s: s.activation(out=rrow[0:1, 1, :], in_=rrow[0:1, 0, :], func=AF.Exp,
                                    scale=csts[-0.5][0:1, :]))
        p.fence("se", "pe")
        p.pe(lambda t: t.matmul(psA[:, :], ones_r[:, :], rrow[0:1, 1, :], start=True, stop=True))
        p.fence("pe", "ve")
        p.ve(lambda v: v.scalar_tensor_tensor(out=tsn[:, :], in0=pp1[:, 0, :],
                                              scalar=lnwd_sb[:, :], in1=psA[:, :],
                                              op0=ALU.mult, op1=ALU.mult))
        p.fence("ve", "se")
        p.se(lambda s: s.activation(out=sq1[0:32, 0:sl], in_=pp2[:, 0, :], func=AF.Square))
        p.fence("se", "pe")
        p.pe(lambda t: t.matmul(psB[0:2, 0:sl], bsel_sb[:, :], sq1[0:32, 0:sl], start=True, stop=True))
        p.fence("pe", "se")
        p.se(lambda s: s.activation(out=rrow2[:, 0, :], in_=psB[0:2, 0:sl], func=AF.Ln,
                                    scale=csts[1.0 / 16.0][0:2, :], bias=csts[EPS][0:2, :]))
        p.se(lambda s: s.activation(out=rrow2[:, 1, :], in_=rrow2[:, 0, :], func=AF.Exp,
                                    scale=csts[-0.5][0:2, :]))
        p.fence("se", "pe")
        p.pe(lambda t: t.matmul(psB[0:32, 0:sl], bsel2_sb[:, :], rrow2[:, 1, :], start=True, stop=True))
        p.fence("pe", "ve")
        p.ve(lambda v: v.scalar_tensor_tensor(out=bcn[:, :], in0=pp2[:, 0, :],
                                              scalar=lnwbc_sb[:, :], in1=psB[0:32, 0:sl],
                                              op0=ALU.mult, op1=ALU.mult))
        p.fence("ve", "sy", "pe")
        p.dma("sy", bcd[:], bcn[:, :])
        p.fence_dma("sy", "sy")
        p.dma("sy", brep[:, :], bass.AP(bcd.tensor, 0, [[0, 128], [512, 16], [1, 512]]))
        p.dma("sy", crep[:, :], bass.AP(bcd.tensor, 16 * 512, [[0, 128], [512, 16], [1, 512]]))

        for m in range(4):
            ps_ = [psC, psD][m % 2]
            p.wait("pe", "ve", max(dt_evac[m % 2], wo_evac[m % 2]))
            p.pe(lambda t, m=m, ps_=ps_: t.matmul(ps_[:, :], wdt_sb[:, m, :], tsn[:, :],
                                                  start=True, stop=True))
            p.fence("pe", "ve")
            p.ve(lambda v, m=m, ps_=ps_: v.tensor_scalar(
                out=delta[:, m, :], in0=ps_[:, :], scalar1=dtb_sb[:, m:m + 1],
                scalar2=None, op0=ALU.add))
            dt_evac[m % 2] = p.mark("ve")
        p.fence("ve", "se")
        dfl = delta[:, :, :].rearrange("p a t -> p (a t)")
        sfl = dsc[:, :, :].rearrange("p a t -> p (a t)")
        p.ve(lambda v: v.tensor_scalar(out=sfl, in0=dfl, scalar1=0.0, scalar2=None, op0=ALU.max))
        p.fence("ve", "se")
        p.se(lambda s: s.activation(out=dfl, in_=dfl, func=AF.Abs))
        p.se(lambda s: s.activation(out=dfl, in_=dfl, func=AF.Exp, scale=-1.0))
        p.se(lambda s: s.activation(out=dfl, in_=dfl, func=AF.Ln, bias=csts[1.0][:, :]))
        p.fence("se", "ve")
        p.ve(lambda v: v.tensor_add(dfl, dfl, sfl))
        p.fence("ve", "se")
        p.fence_dma("sy", "ve")

        if debug and slab == 0:
            p.fence("ve", "sy")
            p.fence("se", "sy")
            p.dma("sy", dbg["tsn"], tsn[:, :])
            p.dma("sy", dbg["bcn"], bcn[:, :])
            p.dma("sy", dbg["delta"].rearrange("(a p) t -> p a t", p=128), delta[:, :, :])
            p.dma("sy", dbg["brep"], brep[:, :])
            p.dma("sy", dbg["crep"], crep[:, :])

        for it in range(4):
            for s2 in range(NSC):
                c0 = s2 * sc
                for n in range(16):
                    p.se(lambda s, it=it, n=n, c0=c0: s.activation(
                        out=dA[:, n * sc:(n + 1) * sc], in_=delta[:, it, c0:c0 + sc],
                        func=AF.Exp, scale=a_sb[:, it, n:n + 1]))
                p.ve(lambda v, it=it, c0=c0: v.tensor_mul(
                    du[:, :], delta[:, it, c0:c0 + sc], h_sb[:, it, c0:c0 + sc]))
                du_bc = bass.AP(du.ap().tensor, du.ap().offset, [du.ap().ap[0], [0, 16], [1, sc]])
                br_sl = bass.AP(brep.ap().tensor, brep.ap().offset + c0,
                                [brep.ap().ap[0], [sl, 16], [1, sc]])
                cr_sl = bass.AP(crep.ap().tensor, crep.ap().offset + c0,
                                [crep.ap().ap[0], [sl, 16], [1, sc]])
                p.ve(lambda v, du_bc=du_bc, br_sl=br_sl: v.tensor_mul(dBu[:, :], du_bc, br_sl))
                p.fence("se", "ve")
                for n in range(16):
                    p.ve(lambda v, n=n, it=it: v.tensor_tensor_scan(
                        s_scan[:, n * sc:(n + 1) * sc], dA[:, n * sc:(n + 1) * sc],
                        dBu[:, n * sc:(n + 1) * sc], state[:, it * 16 + n:it * 16 + n + 1],
                        ALU.mult, ALU.add))
                last_bc = bass.AP(s_scan.ap().tensor, s_scan.ap().offset + sc - 1,
                                  [s_scan.ap().ap[0], [sc, 16]])
                p.ve(lambda v, it=it, last_bc=last_bc: v.tensor_copy(
                    state[:, it * 16:(it + 1) * 16], last_bc))
                if debug and slab == 0 and it == 0 and s2 == 0:
                    p.fence("ve", "sy")
                    p.fence("se", "sy")
                    p.dma("sy", dbg["dA"], dA[:, :])
                    p.dma("sy", dbg["dBu"], dBu[:, :])
                    p.dma("sy", dbg["sscan"], s_scan[:, :])
                    p.fence_dma("sy", "ve")
                p.ve(lambda v, cr_sl=cr_sl: v.tensor_mul(dA[:, :], s_scan[:, :], cr_sl))
                if debug and slab == 0 and it == 0 and s2 == 0:
                    p.fence("ve", "sy")
                    p.dma("sy", dbg["sc_"], dA[:, :])
                    p.fence_dma("sy", "ve")
                p.ve(lambda v: v.tensor_add(tr1[:, :], dA[:, 0:8 * sc], dA[:, 8 * sc:16 * sc]))
                p.mark("ve")
                p.ve(lambda v: v.tensor_add(tr1[:, 0:4 * sc], tr1[:, 0:4 * sc], tr1[:, 4 * sc:8 * sc]))
                p.mark("ve")
                p.ve(lambda v: v.tensor_add(tr1[:, 0:2 * sc], tr1[:, 0:2 * sc], tr1[:, 2 * sc:4 * sc]))
                p.mark("ve")
                p.ve(lambda v: v.tensor_add(tr1[:, 0:sc], tr1[:, 0:sc], tr1[:, sc:2 * sc]))
                if debug and slab == 0 and it == 0 and s2 == 0:
                    p.fence("ve", "sy")
                    p.dma("sy", dbg["tr"], tr1[:, 0:sc])
                    p.fence_dma("sy", "ve")
                p.mark("ve")
                p.ve(lambda v, it=it, c0=c0: v.scalar_tensor_tensor(
                    out=y_sb[:, it, c0:c0 + sc], in0=h_sb[:, it, c0:c0 + sc],
                    scalar=dv_sb[:, it:it + 1], in1=tr1[:, 0:sc],
                    op0=ALU.mult, op1=ALU.add))
                p.fence("ve", "se")
        if debug and slab == 0:
            p.fence("ve", "sy")
            p.dma("sy", dbg["y"].rearrange("(a p) t -> p a t", p=128), y_sb[:, :, :])
        p.fence("ve", "se")
        p.se(lambda s: s.activation(out=yn[:, :, :], in_=g_sb[:, :, :], func=AF.Silu))
        p.fence("se", "ve")
        p.ve(lambda v: v.tensor_mul(y_sb[:, :, :], y_sb[:, :, :], yn[:, :, :]))
        p.ve(lambda v: v.tensor_copy(yn[:, :, :], y_sb[:, :, :]))
        p.fence("ve", "pe", "gp")

        for mo in range(16):
            buf = mo % 2
            p.wait("gp", "pe", wo_use[buf])
            ldv = p.dma("gp", wo_sb[:, buf, :], wo[mo])
            p.wait_dma("pe", "gp", ldv)
            ps_ = [psC, psD][mo % 2]
            p.wait("pe", "ve", max(wo_evac[mo % 2], dt_evac[mo % 2]))
            for k in range(4):
                p.pe(lambda t, k=k, buf=buf, ps_=ps_: t.matmul(
                    ps_[:, :], wo_sb[:, buf, k * 128:(k + 1) * 128], yn[:, k, :],
                    start=(k == 0), stop=(k == 3)))
            wo_use[buf] = p.fence("pe", "ve")
            p.wait_dma("ve", "sy", oo_store[buf])
            p.ve(lambda v, buf=buf, ps_=ps_: v.tensor_copy(oo[:, buf, :], ps_[:, :]))
            wo_evac[mo % 2] = p.fence("ve", "sy")
            oo_store[buf] = p.dma("sy", oOv[:, mo, t0:t0 + sl], oo[:, buf, :])
        barrier(p)

    p.fence_dma("sy", "sy")
    return b.finish()


def host_att1b(inputs, bw, core):
    c = core
    wdtT = np.asarray(bw["L1_attn_WdtT"])
    wdts = stream_lhsT(wdtT[:, c * 512:(c + 1) * 512])
    woT = np.asarray(bw["L1_attn_WoutT"])
    wo = stream_lhsT(woT[c * 512:(c + 1) * 512, :])
    isl = slice(c * 512, (c + 1) * 512)
    A = np.asarray(inputs["L1_attn_A"]).astype(np.float32)[isl]
    bsel = (np.arange(32)[:, None] // 16 == np.arange(2)[None, :]).astype(np.float32)
    return dict(
        wdts=wdts, wo=wo,
        a_in=cd(A.reshape(4, 128, 16).transpose(1, 0, 2)),
        dtb=tile128(np.asarray(inputs["L1_attn_dt_bias"]).astype(np.float32)[isl]),
        dv_in=tile128(np.asarray(inputs["L1_attn_D"]).astype(np.float32)[isl]),
        lnw_dt=np.asarray(inputs["L1_attn_dt_ln_w"]).astype(np.float32)[:, None],
        lnw_bc=np.concatenate([np.asarray(inputs["L1_attn_b_ln_w"]),
                               np.asarray(inputs["L1_attn_c_ln_w"])]).astype(np.float32)[:, None],
        bsel=bsel, bsel2=cdT(bsel),
        onesf=np.ones((128, 1), np.float32), ones_r=np.ones((1, 128), np.float32),
    )



def build_att1c(tok=TOK, tt=512):
    MO = D // 128
    NT = tok // tt
    b = Nc()
    xT = b.inp("xT", [D, tok])
    pO = b.inp("pO", [8, D, tok])
    yT = b.out("yT", [D, tok])
    xTv = xT.rearrange("(c p) t -> p c t", p=128)
    yTv = yT.rearrange("(c p) t -> p c t", p=128)

    x_t = b.sb("x_t", [128, 2, tt])
    acc = b.sb("acc", [128, 2, 8, tt])
    res = b.sb("res", [128, 2, tt])
    p = b.start()
    pOv = [pO[j].rearrange("(c p) t -> p c t", p=128) for j in range(8)]
    st_done = [0, 0]
    for it in range(NT):
        t0 = it * tt
        for mo in range(MO):
            buf = mo % 2
            p.fence("ve", "sy")
            p.dma("sy", x_t[:, buf, :], xTv[:, mo, t0:t0 + tt])
            for j in range(8):
                p.dma("sy", acc[:, buf, j, :], pOv[j][:, mo, t0:t0 + tt])
            p.fence_dma("sy", "ve")
            for half in (4, 2, 1):
                p.ve(lambda v, half=half, buf=buf: v.tensor_add(
                    acc[:, buf, 0:half, :], acc[:, buf, 0:half, :],
                    acc[:, buf, half:2 * half, :]))
                p.mark("ve")
            p.wait_dma("ve", "sy", st_done[buf])
            p.ve(lambda v, buf=buf: v.tensor_add(res[:, buf, :], acc[:, buf, 0, :], x_t[:, buf, :]))
            p.fence("ve", "sy")
            st_done[buf] = p.dma("sy", yTv[:, mo, t0:t0 + tt], res[:, buf, :])
    p.fence_dma("sy", "sy")
    return b.finish()



def _to_featmajor(shards):
    full = np.empty((D, B * L), np.float32)
    for c in range(NCORES):
        bidx = c // (NCORES // B)
        s = (c % (NCORES // B)) * TOK
        full[:, bidx * L + s:bidx * L + s + TOK] = shards[c]
    return full


def _fm_slices(full):
    out = []
    for c in range(NCORES):
        bidx = c // (NCORES // B)
        s = (c % (NCORES // B)) * TOK
        out.append(np.ascontiguousarray(full[:, bidx * L + s:bidx * L + s + TOK]))
    return out


def kernel(**inputs):
    inp = inputs
    bw = run_prep(inp)

    x0 = np.asarray(inp["L0_attn_x"])
    shards = shard_tokens(x0)

    nc_ffn0 = build_ffn0()
    base0 = host_ffn0(inp, bw, "L0_ffn_")
    res = run_launch(nc_ffn0, [dict(base0, xT=shards[c]) for c in range(NCORES)])
    x1s = [r["yT"] for r in res.results]
    x1T = _to_featmajor(x1s)

    nc_att0 = build_att0()
    res = run_launch(nc_att0, [dict(host_att0(inp, bw, c), xT=x1T) for c in range(NCORES)])
    yg = np.concatenate([r["ygT"] for r in res.results], axis=0)
    ssq = np.concatenate([r["ssq"] for r in res.results], axis=0)

    nc_fin = build_att0fin()
    basef = dict(
        nw=tile128(np.asarray(inp["L0_attn_norm_w"]).astype(np.float32)),
        wo=bw["L0_attn_outT"],
        onesf=np.ones((128, 1), np.float32), ones_r=np.ones((1, 128), np.float32),
    )
    in_maps = []
    for c in range(NCORES):
        bidx = c // (NCORES // B)
        s = (c % (NCORES // B)) * TOK
        cols = slice(bidx * L + s, bidx * L + s + TOK)
        in_maps.append(dict(basef, xT=np.ascontiguousarray(x1T[:, cols]),
                            ygT=np.ascontiguousarray(yg[:, cols]),
                            ssq8=np.ascontiguousarray(ssq[:, cols])))
    res = run_launch(nc_fin, in_maps)
    x2s = [r["yT"] for r in res.results]

    res = run_launch(nc_ffn0, [dict(base0, xT=x2s[c]) for c in range(NCORES)])
    x3s = [r["yT"] for r in res.results]

    nc_ffn1 = build_ffn1()
    base1 = host_ffn1(inp, bw)
    res = run_launch(nc_ffn1, [dict(base1, xT=x3s[c]) for c in range(NCORES)])
    x4s = [r["yT"] for r in res.results]
    x4T = _to_featmajor(x4s)

    nc_a1 = build_att1a()
    res = run_launch(nc_a1, [dict(host_att1a(inp, bw, c), xT=x4T) for c in range(NCORES)])
    pD = np.stack([r["pO"] for r in res.results])
    nc_b1 = build_att1b()
    res2 = run_launch(nc_b1, [dict(host_att1b(inp, bw, c), hI=res.results[c]["hO"],
                                   gI=res.results[c]["gO"], pI=pD)
                              for c in range(NCORES)])
    oD = np.stack([r["oO"] for r in res2.results])
    nc_c1 = build_att1c()
    in_maps = []
    for c in range(NCORES):
        bidx = c // (NCORES // B)
        s = (c % (NCORES // B)) * TOK
        cols = slice(bidx * L + s, bidx * L + s + TOK)
        in_maps.append(dict(xT=x4s[c], pO=np.ascontiguousarray(oD[:, :, cols])))
    res = run_launch(nc_c1, in_maps)
    x5s = [r["yT"] for r in res.results]

    res = run_launch(nc_ffn1, [dict(base1, xT=x5s[c]) for c in range(NCORES)])
    x6s = [r["yT"] for r in res.results]

    return unshard_tokens(x6s)


# version 04 (final)
# speedup vs baseline: 1.0059x; 1.0059x over previous
import contextlib
import numpy as np
import ml_dtypes
import concourse.bass as bass
import concourse.mybir as mybir
from concourse.bass_utils import run_bass_kernel_spmd

F32 = mybir.dt.float32
BF16 = mybir.dt.bfloat16
AF = mybir.ActivationFunctionType
ALU = mybir.AluOpType
AX = mybir.AxisListType

BF = ml_dtypes.bfloat16

B, L, D = 2, 8192, 2048
NCORES = 8
TOK = (B * L) // NCORES
EPS = 1e-5



def cdT(x):
    return np.ascontiguousarray(np.asarray(x).T)


def cd(x):
    return np.ascontiguousarray(np.asarray(x))


def tile128(v):
    v = np.asarray(v)
    return np.ascontiguousarray(v.reshape(-1, 128).T)


def stream_lhsT(WT):
    K, M = WT.shape
    KC, MT = K // 128, M // 128
    return np.ascontiguousarray(
        WT.reshape(KC, 128, MT, 128).transpose(2, 1, 0, 3).reshape(MT, 128, KC * 128))



ENGS = ("pe", "ve", "se", "gp", "sy")
_ENG_ATTR = {"pe": "tensor", "ve": "vector", "se": "scalar", "gp": "gpsimd", "sy": "sync"}


class Prog:
    def __init__(self, nc, stack):
        self.nc = nc
        self.q = {e: [] for e in ENGS}
        self.cnt = {e: 0 for e in ENGS}
        self.marked = {e: True for e in ENGS}
        self.dcnt = {e: 0 for e in ENGS}
        self.sem_c = {e: stack.enter_context(nc.semaphore(f"sc_{e}")) for e in ENGS}
        self.sem_d = {e: stack.enter_context(nc.semaphore(f"sd_{e}"))
                      for e in ("sy", "gp", "se", "ve")}
        self.waited = {}

    def op(self, e, fn):
        self.q[e].append([fn, "op"])
        self.marked[e] = False

    def pe(self, fn): self.op("pe", fn)
    def ve(self, fn): self.op("ve", fn)
    def se(self, fn): self.op("se", fn)
    def gp(self, fn): self.op("gp", fn)
    def sy(self, fn): self.op("sy", fn)

    def dma(self, e, out, in_, **kw):
        sem = self.sem_d[e]
        def fn(eng, out=out, in_=in_, kw=kw, sem=sem):
            return eng.dma_start(out=out, in_=in_, **kw).then_inc(sem, 16)
        self.q[e].append([fn, "dma"])
        self.dcnt[e] += 16
        return self.dcnt[e]

    def mark(self, e):
        if self.marked[e]:
            return self.cnt[e]
        sem = self.sem_c[e]
        self.cnt[e] += 1
        def fn(eng, sem=sem):
            return eng.drain(fusable=False).then_inc(sem, 1)
        self.q[e].append([fn, "marked"])
        self.marked[e] = True
        return self.cnt[e]

    def wait(self, cons, e, val):
        if val <= 0:
            return
        key = (cons, "c", e)
        if self.waited.get(key, 0) >= val:
            return
        sem = self.sem_c[e]
        self.q[cons].append([lambda eng, sem=sem, val=val: eng.wait_ge(sem, val), "wait"])
        self.waited[key] = val

    def wait_dma(self, cons, e, val):
        if val <= 0:
            return
        key = (cons, "d", e)
        if self.waited.get(key, 0) >= val:
            return
        sem = self.sem_d[e]
        self.q[cons].append([lambda eng, sem=sem, val=val: eng.wait_ge(sem, val), "wait"])
        self.waited[key] = val

    def fence(self, prod, *cons):
        v = self.mark(prod)
        for c in cons:
            self.wait(c, prod, v)
        return v

    def fence_dma(self, e, *cons):
        v = self.dcnt[e]
        for c in cons:
            self.wait_dma(c, e, v)
        return v

    def emit(self, blk):
        for e in ENGS:
            ops = self.q[e]
            if not ops:
                continue
            def run(eng, ops=ops):
                for rec in ops:
                    rec[0](eng)
            getattr(blk, _ENG_ATTR[e])(run)


class Nc:
    def __init__(self):
        self.nc = bass.Bass("TRN2", target_bir_lowering=False)
        self.stack = contextlib.ExitStack()
        self.p = None

    def inp(self, name, shape, dt=F32):
        return self.nc.dram_tensor(name, list(shape), dt, kind="ExternalInput").ap()

    def out(self, name, shape, dt=F32):
        return self.nc.dram_tensor(name, list(shape), dt, kind="ExternalOutput").ap()

    def scratch(self, name, shape, dt=F32):
        return self.nc.dram_tensor(name, list(shape), dt).ap()

    def sb(self, name, shape, dt=F32):
        return self.stack.enter_context(self.nc.sbuf_tensor(name, list(shape), dt))

    def ps(self, name, shape, dt=F32):
        return self.stack.enter_context(self.nc.psum_tensor(name, list(shape), dt))

    def start(self):
        self.p = Prog(self.nc, self.stack)
        return self.p

    def finish(self):
        blk = self.stack.enter_context(self.nc.Block())
        self.p.emit(blk)
        self.stack.close()
        return self.nc


def free_bcast(tile_ap, reps, tt):
    a = tile_ap
    return bass.AP(a.tensor, a.offset, [a.ap[0], [0, reps], [1, tt]])


_launch_count = 0


def run_launch(nc, in_maps, **kw):
    global _launch_count
    _launch_count += 1
    return run_bass_kernel_spmd(nc, in_maps, core_ids=list(range(len(in_maps))), **kw)


def make_consts(b, p, values):
    vals = sorted(set(float(v) for v in values))
    cst = b.sb("cstpool", [128, max(1, len(vals))])
    for i, v in enumerate(vals):
        p.ve(lambda ve, i=i, v=v: ve.memset(cst[:, i:i + 1], v))
    p.fence("ve", "se", "pe", "gp")
    return {v: cst[:, i:i + 1] for i, v in enumerate(vals)}



def emit_ln_rows(p, x_t, sq, KC, tt, onesf, ps_misc, rows, csts):
    p.se(lambda s: s.activation(out=sq[:, :, :], in_=x_t[:, :, :], func=AF.Square))
    p.fence("se", "pe")
    for k in range(KC):
        p.pe(lambda t, k=k: t.matmul(ps_misc[0:1, 0:tt], onesf[:, :], x_t[:, k, :],
                                     start=(k == 0), stop=(k == KC - 1)))
    for k in range(KC):
        p.pe(lambda t, k=k: t.matmul(ps_misc[32:33, 0:tt], onesf[:, :], sq[:, k, :],
                                     start=(k == 0), stop=(k == KC - 1)))
    p.fence("pe", "se", "ve")
    inv_n = 1.0 / (KC * 128)
    p.se(lambda s: s.mul(rows[0:1, 0, :], ps_misc[0:1, 0:tt], inv_n))
    p.se(lambda s: s.activation(out=rows[0:1, 2, :], in_=rows[0:1, 0, :], func=AF.Square))
    p.fence("se", "ve")
    p.ve(lambda v: v.scalar_tensor_tensor(
        out=rows[0:1, 3, :], in0=ps_misc[32:33, 0:tt], scalar=inv_n,
        in1=rows[0:1, 2, :], op0=ALU.mult, op1=ALU.subtract))
    p.fence("ve", "se")
    p.se(lambda s: s.activation(out=rows[0:1, 3, :], in_=rows[0:1, 3, :], func=AF.Ln,
                                bias=csts[EPS][0:1, :]))
    p.se(lambda s: s.activation(out=rows[0:1, 1, :], in_=rows[0:1, 3, :], func=AF.Exp,
                                scale=csts[-0.5][0:1, :]))



def build_ffn0(tok=TOK, tt=512):
    KC = D // 128
    MH = 4096 // 128
    MO = D // 128
    NT = tok // tt

    b = Nc()
    xT = b.inp("xT", [D, tok])
    wg = b.inp("wg", [D, 2])
    w0a = b.inp("w0a", [MH, 128, KC * 128], BF16)
    w1a = b.inp("w1a", [MH, 128, KC * 128], BF16)
    w0b = b.inp("w0b", [MO, 128, MH * 128], BF16)
    w1b = b.inp("w1b", [MO, 128, MH * 128], BF16)
    b0a = b.inp("b0a", [128, MH])
    b1a = b.inp("b1a", [128, MH])
    b0b = b.inp("b0b", [128, MO])
    b1b = b.inp("b1b", [128, MO])
    onesf_d = b.inp("onesf", [128, 1])
    ones_r_d = b.inp("ones_r", [1, 128])
    pm1_d = b.inp("pm1", [2, 1])
    yT = b.out("yT", [D, tok])

    xTv = xT.rearrange("(c p) t -> p c t", p=128)
    yTv = yT.rearrange("(c p) t -> p c t", p=128)
    wgv = wg.rearrange("(c p) m -> p c m", p=128)

    x_t = b.sb("x_t", [128, KC, tt])
    sq = b.sb("sq", [128, KC, tt])
    u = b.sb("u", [128, KC, tt], BF16)
    h0 = b.sb("h0", [128, MH, tt], BF16)
    h1 = b.sb("h1", [128, MH, tt], BF16)
    rows = b.sb("rows", [1, 6, tt])
    grow = b.sb("grow", [2, tt])
    c1 = b.sb("c1", [128, tt])
    dscr = b.sb("dscr", [128, tt])
    res = b.sb("res", [128, 2, tt])
    onesf = b.sb("onesf_sb", [128, 1])
    ones_r = b.sb("ones_r_sb", [1, 128])
    pm1 = b.sb("pm1_sb", [2, 1])
    wg_sb = b.sb("wg_sb", [128, KC, 2])
    b0a_sb = b.sb("b0a_sb", [128, MH])
    b1a_sb = b.sb("b1a_sb", [128, MH])
    b0b_sb = b.sb("b0b_sb", [128, MO])
    b1b_sb = b.sb("b1b_sb", [128, MO])
    gsd = b.sb("gsd", [1, 1])
    wA = b.sb("wA", [128, 2, KC * 128], BF16)
    wB0 = b.sb("wB0", [128, 2, MH * 128], BF16)
    wB1 = b.sb("wB1", [128, 2, MH * 128], BF16)

    ps_h = [b.ps("ps_h0", [128, tt]), b.ps("ps_h1", [128, tt])]
    ps_o = [b.ps("ps_o0", [128, tt]), b.ps("ps_o1", [128, tt])]
    ps_b = [b.ps("ps_b0", [128, tt]), b.ps("ps_b1", [128, tt])]
    ps_misc = b.ps("ps_misc", [128, tt])

    p = b.start()
    csts = make_consts(b, p, [EPS, -0.5])

    p.dma("sy", onesf[:, :], onesf_d)
    p.dma("sy", ones_r[:, :], ones_r_d)
    p.dma("sy", pm1[:, :], pm1_d)
    p.dma("sy", wg_sb[:, :, :], wgv)
    p.dma("sy", b0a_sb[:, :], b0a)
    p.dma("sy", b1a_sb[:, :], b1a)
    p.dma("sy", b0b_sb[:, :], b0b)
    p.dma("sy", b1b_sb[:, :], b1b)
    p.fence_dma("sy", "pe", "ve", "se")

    for k in range(KC):
        p.pe(lambda t, k=k: t.matmul(ps_misc[64:66, 0:1], wg_sb[:, k, :], onesf[:, :],
                                     start=(k == 0), stop=(k == KC - 1)))
    p.fence("pe", "ve")
    p.ve(lambda v: v.tensor_copy(grow[0:2, 0:1], ps_misc[64:66, 0:1]))
    p.fence("ve", "pe")
    p.pe(lambda t: t.matmul(ps_misc[0:1, 0:1], pm1[:, :], grow[0:2, 0:1], start=True, stop=True))
    p.fence("pe", "ve")
    p.ve(lambda v: v.tensor_copy(gsd[:, :], ps_misc[0:1, 0:1]))
    p.fence("ve", "pe")

    h_evac = [0, 0]
    o_evac_ve = 0
    o_evac_se = 0
    wA_use = [0, 0]
    wB_use = [0, 0]
    res_store = [0, 0]

    for it in range(NT):
        t0 = it * tt
        if it > 0:
            p.fence("ve", "sy")
        p.dma("sy", x_t[:, :, :], xTv[:, :, t0:t0 + tt])
        p.fence_dma("sy", "se", "pe", "ve")
        if it > 0:
            p.fence("ve", "pe")

        emit_ln_rows(p, x_t, sq, KC, tt, onesf, ps_misc, rows, csts)
        p.fence("se", "pe", "ve")

        for k in range(KC):
            p.pe(lambda t, k=k: t.matmul(ps_misc[64:66, 0:tt], wg_sb[:, k, :], x_t[:, k, :],
                                         start=(k == 0), stop=(k == KC - 1)))
        p.fence("pe", "ve")
        p.ve(lambda v: v.tensor_copy(grow[:, :], ps_misc[64:66, 0:tt]))
        p.fence("ve", "pe")
        p.pe(lambda t: t.matmul(ps_misc[0:1, 0:tt], pm1[:, :], grow[:, :], start=True, stop=True))
        p.fence("pe", "ve")
        p.ve(lambda v: v.scalar_tensor_tensor(
            out=rows[0:1, 4, :], in0=rows[0:1, 0, :], scalar=gsd[0:1, 0:1],
            in1=ps_misc[0:1, 0:tt], op0=ALU.mult, op1=ALU.subtract))
        p.ve(lambda v: v.tensor_single_scalar(
            out=rows[0:1, 5, :], in_=rows[0:1, 4, :], scalar=0.0, op=ALU.is_le))
        mask_ready = p.fence("ve", "pe")

        p.pe(lambda t: t.matmul(ps_b[0][:, :], ones_r[:, :], rows[0:1, 0, :], start=True, stop=True))
        p.pe(lambda t: t.matmul(ps_b[1][:, :], ones_r[:, :], rows[0:1, 1, :], start=True, stop=True))
        p.fence("pe", "ve")

        mean_b = free_bcast(ps_b[0][:, 0:1], KC, tt)
        rstd_b = free_bcast(ps_b[1][:, 0:1], KC, tt)
        p.ve(lambda v: v.tensor_sub(sq[:, :, :], x_t[:, :, :], mean_b))
        p.ve(lambda v: v.tensor_mul(u[:, :, :], sq[:, :, :], rstd_b))
        p.fence("ve", "pe", "gp")

        p.pe(lambda t: t.matmul(ps_b[0][:, :], ones_r[:, :], rows[0:1, 5, :], start=True, stop=True))
        p.mark("pe")

        for e in range(2):
            wsrc = w0a if e == 0 else w1a
            hdst = h0 if e == 0 else h1
            bias = b0a_sb if e == 0 else b1a_sb
            act = AF.Gelu if e == 0 else AF.Relu
            for m in range(MH):
                gi = e * MH + m
                buf = gi % 2
                p.wait("gp", "pe", wA_use[buf])
                ldv = p.dma("gp", wA[:, buf, :], wsrc[m])
                p.wait_dma("pe", "gp", ldv)
                pb = gi % 2
                p.wait("pe", "se", h_evac[pb])
                for k in range(KC):
                    p.pe(lambda t, k=k, pb=pb, buf=buf: t.matmul(
                        ps_h[pb][:, :], wA[:, buf, k * 128:(k + 1) * 128], u[:, k, :],
                        start=(k == 0), stop=(k == KC - 1)))
                wA_use[buf] = p.fence("pe", "se")
                p.se(lambda s, m=m, pb=pb, hdst=hdst, bias=bias, act=act: s.activation(
                    out=hdst[:, m, :], in_=ps_h[pb][:, :], func=act, bias=bias[:, m:m + 1]))
                h_evac[pb] = p.mark("se")
        p.fence("se", "pe")

        for mo in range(MO):
            buf = mo % 2
            p.wait("gp", "pe", wB_use[buf])
            p.dma("gp", wB0[:, buf, :], w0b[mo])
            ldv = p.dma("gp", wB1[:, buf, :], w1b[mo])
            p.wait_dma("pe", "gp", ldv)
            p.wait("pe", "ve", o_evac_ve)
            p.wait("pe", "se", o_evac_se)
            for k in range(MH):
                p.pe(lambda t, k=k, buf=buf: t.matmul(
                    ps_o[0][:, :], wB0[:, buf, k * 128:(k + 1) * 128], h0[:, k, :],
                    start=(k == 0), stop=(k == MH - 1)))
            for k in range(MH):
                p.pe(lambda t, k=k, buf=buf: t.matmul(
                    ps_o[1][:, :], wB1[:, buf, k * 128:(k + 1) * 128], h1[:, k, :],
                    start=(k == 0), stop=(k == MH - 1)))
            wB_use[buf] = p.fence("pe", "se", "ve")
            p.wait("se", "ve", o_evac_ve)
            p.se(lambda s, mo=mo: s.activation(out=c1[:, :], in_=ps_o[1][:, :],
                                               func=AF.Identity, bias=b1b_sb[:, mo:mo + 1]))
            o_evac_se = p.fence("se", "ve")
            p.ve(lambda v, mo=mo: v.scalar_tensor_tensor(
                out=dscr[:, :], in0=ps_o[0][:, :], scalar=b0b_sb[:, mo:mo + 1],
                in1=c1[:, :], op0=ALU.add, op1=ALU.subtract))
            p.ve(lambda v: v.tensor_mul(dscr[:, :], dscr[:, :], ps_b[0][:, :]))
            p.ve(lambda v: v.tensor_add(dscr[:, :], dscr[:, :], c1[:, :]))
            p.wait_dma("ve", "sy", res_store[buf])
            p.ve(lambda v, mo=mo, buf=buf: v.scalar_tensor_tensor(
                out=res[:, buf, :], in0=dscr[:, :], scalar=0.5,
                in1=x_t[:, mo, :], op0=ALU.mult, op1=ALU.add))
            o_evac_ve = p.fence("ve", "sy")
            res_store[buf] = p.dma("sy", yTv[:, mo, t0:t0 + tt], res[:, buf, :])

    p.fence_dma("sy", "sy")
    return b.finish()


def host_ffn0(inputs, bw, prefix):
    g = lambda k: np.asarray(inputs[prefix + k])
    return dict(
        wg=cdT(g("Wg")),
        w0a=bw[prefix + "W0a"], w1a=bw[prefix + "W1a"],
        w0b=bw[prefix + "W0b"], w1b=bw[prefix + "W1b"],
        b0a=tile128(g("b0a")), b1a=tile128(g("b1a")),
        b0b=tile128(g("b0b")), b1b=tile128(g("b1b")),
        onesf=np.ones((128, 1), np.float32),
        ones_r=np.ones((1, 128), np.float32),
        pm1=np.array([[1.0], [-1.0]], np.float32),
    )


def shard_tokens(x):
    out = []
    for c in range(NCORES):
        bidx = c // (NCORES // B)
        s = (c % (NCORES // B)) * TOK
        out.append(cdT(x[bidx, s:s + TOK, :]))
    return out


def unshard_tokens(parts):
    x = np.empty((B, L, D), np.float32)
    for c in range(NCORES):
        bidx = c // (NCORES // B)
        s = (c % (NCORES // B)) * TOK
        x[bidx, s:s + TOK, :] = parts[c].T
    return x




PREP_TILE = 128 * 16384


def build_prep2(nblob_tiles):
    b = Nc()
    blob = b.inp("blob", [nblob_tiles * PREP_TILE])
    w2Ts = b.inp("w2Ts", [8192, 256])
    h128 = b.inp("h128", [128, 128], BF16)
    h64 = b.inp("h64", [64, 64], BF16)
    blob_o = b.out("blob_o", [nblob_tiles * PREP_TILE], BF16)
    w2p = b.out("w2p", [8192, 256], BF16)
    waT = b.scratch("waT", [8192, 256], BF16)

    blv = blob.rearrange("(n p t) -> n p t", p=128, t=16384)
    bov = blob_o.rearrange("(n p t) -> n p t", p=128, t=16384)
    w2v = w2Ts.rearrange("(f p) d -> f p d", p=128)
    wav = waT.rearrange("(f p) d -> f p d", p=128)

    cb = b.sb("cb", [128, 2, 16384], BF16)
    h128_sb = b.sb("h128_sb", [128, 128], BF16)
    h64_sb = b.sb("h64_sb", [64, 64], BF16)
    wblk = b.sb("wblk", [128, 2, 256], BF16)
    wa_sb = b.sb("wa_sb", [128, 2, 256], BF16)
    comb = b.sb("comb", [64, 2, 256], BF16)
    w2p_sb = b.sb("w2p_sb", [64, 2, 256], BF16)
    ps = b.ps("ps", [128, 2, 256])

    p = b.start()
    csts = make_consts(b, p, [2.0 ** -6.5])
    p.dma("sy", h128_sb[:, :], h128)
    p.dma("sy", h64_sb[:, :], h64)

    st_done = [0, 0]
    for i in range(nblob_tiles):
        bi = i % 2
        p.wait_dma("gp", "sy", st_done[bi])
        ld = p.dma("gp", cb[:, bi, :], blv[i])
        p.wait_dma("sy", "gp", ld)
        st_done[bi] = p.dma("sy", bov[i], cb[:, bi, :])

    evacd = [0, 0]
    stored = [0, 0]
    for f in range(64):
        bi = f % 2
        p.wait("gp", "ve", evacd[bi])
        ld = p.dma("gp", wblk[:, bi, :], w2v[f])
        p.wait_dma("pe", "gp", ld)
        p.wait("pe", "ve", evacd[bi])
        p.pe(lambda t, bi=bi: t.matmul(ps[:, bi, :], h128_sb[:, :], wblk[:, bi, :],
                                       start=True, stop=True))
        p.fence("pe", "ve")
        p.wait_dma("ve", "sy", stored[bi])
        p.ve(lambda v, bi=bi: v.tensor_copy(wa_sb[:, bi, :], ps[:, bi, :]))
        evacd[bi] = p.fence("ve", "sy", "pe", "gp")
        stored[bi] = p.dma("sy", wav[f], wa_sb[:, bi, :])
    p.fence_dma("sy", "gp")

    evacd = [0, 0]
    stored = [0, 0]
    for g in range(128):
        bi = g % 2
        src = bass.AP(waT.tensor, g * 256, [[128 * 256, 64], [1, 256]])
        p.wait("gp", "se", evacd[bi])
        ld = p.dma("gp", comb[:, bi, :], src)
        p.wait_dma("pe", "gp", ld)
        p.wait("pe", "se", evacd[bi])
        p.pe(lambda t, bi=bi: t.matmul(ps[0:64, bi, :], h64_sb[:, :], comb[:, bi, :],
                                       start=True, stop=True))
        p.fence("pe", "se")
        p.wait_dma("se", "sy", stored[bi])
        p.se(lambda s, bi=bi: s.activation(out=w2p_sb[:, bi, :], in_=ps[0:64, bi, :],
                                           func=AF.Identity,
                                           scale=csts[2.0 ** -6.5][0:64, :]))
        evacd[bi] = p.fence("se", "sy", "pe", "gp")
        dst = bass.AP(w2p.tensor, g * 256, [[128 * 256, 64], [1, 256]])
        stored[bi] = p.dma("sy", dst, w2p_sb[:, bi, :])
    p.fence_dma("sy", "sy")
    return b.finish()


def make_bf16_manifest(inputs):
    man = []
    def add(key, arr):
        man.append((key, np.asarray(arr, np.float32)))
    add("L0_ffn_W0a", stream_lhsT(cdT(inputs["L0_ffn_W0a"])))
    add("L0_ffn_W1a", stream_lhsT(cdT(inputs["L0_ffn_W1a"])))
    add("L0_ffn_W0b", stream_lhsT(cdT(inputs["L0_ffn_W0b"])))
    add("L0_ffn_W1b", stream_lhsT(cdT(inputs["L0_ffn_W1b"])))
    add("L1_ffn_W1", stream_lhsT(cdT(inputs["L1_ffn_W1"])))
    add("L1_ffn_W3", stream_lhsT(cdT(inputs["L1_ffn_W3"])))
    add("L0_attn_inT", cdT(inputs["L0_attn_in_proj_w"]))
    add("L0_attn_outT", stream_lhsT(cdT(inputs["L0_attn_out_proj_w"])))
    add("L1_attn_W_inT", cdT(inputs["L1_attn_W_in"]))
    add("L1_attn_WdtT", cdT(inputs["L1_attn_Wdt"]))
    add("L1_attn_WoutT", cdT(inputs["L1_attn_Wout"]))
    return man


def run_prep(inputs, extra_man=()):
    man = make_bf16_manifest(inputs) + list(extra_man)
    flat = [a.ravel() for _, a in man]
    total = int(sum(a.size for a in flat))
    per = -(-total // (NCORES * PREP_TILE)) * PREP_TILE
    blob = np.zeros(NCORES * per, np.float32)
    blob[:total] = np.concatenate(flat)
    w2T = cdT(inputs["L1_ffn_W2"])
    h128m = _hmat(128).astype(BF)
    h64m = _hmat(64).astype(BF)

    nc = build_prep2(per // PREP_TILE)
    in_maps = []
    for c in range(NCORES):
        in_maps.append(dict(
            blob=blob[c * per:(c + 1) * per],
            w2Ts=cd(w2T[:, c * 256:(c + 1) * 256]),
            h128=h128m, h64=h64m,
        ))
    res = run_launch(nc, in_maps)
    blob_o = np.concatenate([r["blob_o"] for r in res.results])
    out = {}
    off = 0
    for key, a in man:
        out[key] = blob_o[off:off + a.size].reshape(a.shape)
        off += a.size
    w2pT = np.concatenate([r["w2p"] for r in res.results], axis=1)
    out["W2P"] = stream_lhsT(w2pT)
    return out


def _hmat(n):
    i = np.arange(n)
    pc = np.bitwise_count((i[:, None] & i[None, :]).astype(np.int64)).astype(np.int64)
    return np.where(pc % 2 == 0, np.float32(1.0), np.float32(-1.0))



def build_ffn1(tok=TOK, tt=512, debug=False):
    KC = D // 128
    MH = 8192 // 128
    MO = D // 128
    NT = tok // tt

    b = Nc()
    xT = b.inp("xT", [D, tok])
    w1 = b.inp("w1", [MH, 128, KC * 128], BF16)
    w3 = b.inp("w3", [MH, 128, KC * 128], BF16)
    w2p = b.inp("w2p", [MO, 128, MH * 128], BF16)
    onesf_d = b.inp("onesf", [128, 1])
    ones_r_d = b.inp("ones_r", [1, 128])
    yT = b.out("yT", [D, tok])

    xTv = xT.rearrange("(c p) t -> p c t", p=128)
    yTv = yT.rearrange("(c p) t -> p c t", p=128)

    x_t = b.sb("x_t", [128, KC, tt])
    sq = b.sb("sq", [128, KC, tt])
    u = b.sb("u", [128, KC, tt], BF16)
    hid = b.sb("hid", [128, MH, tt], BF16)
    rows = b.sb("rows", [1, 4, tt])
    sg = b.sb("sg", [128, tt], BF16)
    res = b.sb("res", [128, 2, tt])
    onesf = b.sb("onesf_sb", [128, 1])
    ones_r = b.sb("ones_r_sb", [1, 128])
    wA1 = b.sb("wA1", [128, 2, KC * 128], BF16)
    wA3 = b.sb("wA3", [128, 2, KC * 128], BF16)
    wB = b.sb("wB", [128, 2, MH * 128], BF16)

    ps_h = [b.ps("ps_h0", [128, tt]), b.ps("ps_h1", [128, tt])]
    ps_o0 = b.ps("ps_o0", [128, tt])
    ps_o1 = b.ps("ps_o1", [128, tt])
    ps_o = [ps_o0, ps_o1]
    ps_b = [b.ps("ps_b0", [128, tt]), b.ps("ps_b1", [128, tt])]
    ps_misc = b.ps("ps_misc", [128, tt])

    if debug:
        dbg_u = b.out("dbg_u", [D, tt], BF16)
        dbg_hid = b.out("dbg_hid", [8192, tt], BF16)
        dbg_hid2 = b.out("dbg_hid2", [8192, tt], BF16)
        dbg_ps = b.out("dbg_ps", [128, tt])
        dbg_uv = dbg_u.rearrange("(c p) t -> p c t", p=128)
        dbg_hv = dbg_hid.rearrange("(c p) t -> p c t", p=128)
        dbg_hv2 = dbg_hid2.rearrange("(c p) t -> p c t", p=128)

    p = b.start()
    csts = make_consts(b, p, [EPS, -0.5])
    p.dma("sy", onesf[:, :], onesf_d)
    p.dma("sy", ones_r[:, :], ones_r_d)
    p.fence_dma("sy", "pe", "ve", "se")

    h_evac_last = 0
    o_evac_ve = 0
    wA_use = [0, 0]
    wB_use = [0, 0]
    res_store = [0, 0]

    for it in range(NT):
        t0 = it * tt
        if it > 0:
            p.fence("ve", "sy")
        p.dma("sy", x_t[:, :, :], xTv[:, :, t0:t0 + tt])
        p.fence_dma("sy", "se", "pe", "ve")
        if it > 0:
            p.fence("ve", "pe")

        emit_ln_rows(p, x_t, sq, KC, tt, onesf, ps_misc, rows, csts)
        p.fence("se", "pe", "ve")

        p.pe(lambda t: t.matmul(ps_b[0][:, :], ones_r[:, :], rows[0:1, 0, :], start=True, stop=True))
        p.pe(lambda t: t.matmul(ps_b[1][:, :], ones_r[:, :], rows[0:1, 1, :], start=True, stop=True))
        p.fence("pe", "ve")

        mean_b = free_bcast(ps_b[0][:, 0:1], KC, tt)
        rstd_b = free_bcast(ps_b[1][:, 0:1], KC, tt)
        p.ve(lambda v: v.tensor_sub(sq[:, :, :], x_t[:, :, :], mean_b))
        p.ve(lambda v: v.tensor_mul(u[:, :, :], sq[:, :, :], rstd_b))
        p.fence("ve", "pe", "gp")

        if debug and it == 0:
            p.dma("sy", dbg_uv[:, :, :], u[:, :, :])

        for m in range(MH):
            buf = m % 2
            p.wait("gp", "pe", wA_use[buf])
            p.dma("gp", wA1[:, buf, :], w1[m])
            ldv = p.dma("gp", wA3[:, buf, :], w3[m])
            p.wait_dma("pe", "gp", ldv)
            p.wait("pe", "ve", h_evac_last)
            for k in range(KC):
                p.pe(lambda t, k=k, buf=buf: t.matmul(
                    ps_h[0][:, :], wA1[:, buf, k * 128:(k + 1) * 128], u[:, k, :],
                    start=(k == 0), stop=(k == KC - 1)))
            for k in range(KC):
                p.pe(lambda t, k=k, buf=buf: t.matmul(
                    ps_h[1][:, :], wA3[:, buf, k * 128:(k + 1) * 128], u[:, k, :],
                    start=(k == 0), stop=(k == KC - 1)))
            wA_use[buf] = p.fence("pe", "se")
            p.wait("se", "ve", h_evac_last)
            p.se(lambda s: s.activation(out=sg[:, :], in_=ps_h[0][:, :], func=AF.Silu))
            p.fence("se", "ve")
            p.ve(lambda v, m=m: v.tensor_mul(hid[:, m, :], sg[:, :], ps_h[1][:, :]))
            h_evac_last = p.fence("ve", "pe")

        if debug and it == 0:
            p.fence("ve", "sy")
            p.dma("sy", dbg_hv[:, :, :], hid[:, :, :])

        for mo in range(MO):
            buf = mo % 2
            p.wait("gp", "pe", wB_use[buf])
            ldv = p.dma("gp", wB[:, buf, :], w2p[mo])
            p.wait_dma("pe", "gp", ldv)
            p.wait("pe", "ve", o_evac_ve)
            for k in range(MH):
                p.pe(lambda t, k=k, buf=buf, mo=mo: t.matmul(
                    ps_o[mo % 2][:, :], wB[:, buf, k * 128:(k + 1) * 128], hid[:, k, :],
                    start=(k == 0), stop=(k == MH - 1)))
            wB_use[buf] = p.fence("pe", "ve")
            if debug and it == 0 and mo == 0:
                p.ve(lambda v: v.tensor_copy(sq[:, 0, :], ps_o[0][:, :]))
                p.fence("ve", "sy")
                p.dma("sy", dbg_ps, sq[:, 0, :])
            p.wait_dma("ve", "sy", res_store[buf])
            p.ve(lambda v, mo=mo, buf=buf: v.scalar_tensor_tensor(
                out=res[:, buf, :], in0=ps_o[mo % 2][:, :], scalar=0.5,
                in1=x_t[:, mo, :], op0=ALU.mult, op1=ALU.add))
            o_evac_ve = p.fence("ve", "sy")
            res_store[buf] = p.dma("sy", yTv[:, mo, t0:t0 + tt], res[:, buf, :])

        if debug and it == 0:
            p.fence("pe", "sy")
            p.dma("sy", dbg_hv2[:, :, :], hid[:, :, :])

    p.fence_dma("sy", "sy")
    return b.finish()


def host_ffn1(inputs, bw):
    return dict(
        w1=bw["L1_ffn_W1"], w3=bw["L1_ffn_W3"], w2p=bw["W2P"],
        onesf=np.ones((128, 1), np.float32),
        ones_r=np.ones((1, 128), np.float32),
    )





def barrier(p, engines=("pe", "ve", "se")):
    vals = {e: p.mark(e) for e in engines}
    for c in engines:
        for e in engines:
            if e != c:
                p.wait(c, e, vals[e])


def build_att0(sl=512, debug=False):
    KC = D // 128
    NSLAB = (B * L) // sl
    NCH = sl // 64

    b = Nc()
    xT = b.inp("xT", [D, B * L])
    wz = b.inp("wz", [4, 128, KC * 128], BF16)
    wx = b.inp("wx", [4, 128, KC * 128], BF16)
    wbc = b.inp("wbc", [2, 128, KC * 128], BF16)
    wdt = b.inp("wdt", [128, KC, 8], BF16)
    wcv = b.inp("wcv", [128, 6, 4])
    cbv = b.inp("cbv", [128, 6])
    alog = b.inp("alog", [8, 1])
    dtb = b.inp("dtb", [8, 1])
    drep = b.inp("drep", [128, 4])
    sel8 = b.inp("sel8", [8, 128])
    cmask8 = b.inp("cmask8", [8, 4])
    maskT_d = b.inp("maskT", [64, 64])
    identb_d = b.inp("identb", [128, 128], BF16)
    identf_d = b.inp("identf", [128, 128])
    onesf_d = b.inp("onesf", [128, 1])
    ones_r_d = b.inp("ones_r", [1, 128])
    ygT = b.out("ygT", [512, B * L])
    ssq = b.out("ssq", [1, B * L])
    cumd = b.scratch("cumd", [NCH * 8 * 64])

    xTv = xT.rearrange("(c p) t -> p c t", p=128)
    ygv = ygT.rearrange("(c p) t -> p c t", p=128)

    x_t = b.sb("x_t", [128, KC, sl])
    sq = b.sb("sq", [128, KC, sl])
    u = b.sb("u", [128, KC, sl], BF16)
    rows = b.sb("rows", [1, 4, sl])
    z_sb = b.sb("z_sb", [128, 4, sl], BF16)
    xsp = b.sb("xsp", [128, 6, sl + 3], BF16)
    cv = b.sb("cv", [128, sl])
    xc = b.sb("xc", [128, 6, sl], BF16)
    dtr = b.sb("dtr", [8, 4, sl])
    cum = b.sb("cum", [8, sl])
    cumflat = b.sb("cumflat", [1, NCH * 8 * 64])
    z64 = b.sb("z64", [8, 64])
    XT_sb = b.sb("XT_sb", [64, 512], BF16)
    BT_sb = b.sb("BT_sb", [64, 128], BF16)
    mt_sb = b.sb("mt_sb", [64, 512], BF16)
    dts = b.sb("dts", [64, 512])
    xft = b.sb("xft", [64, 512], BF16)
    ct_sb = b.sb("ct_sb", [128, 512], BF16)
    cbm = b.sb("cbm", [64, 64])
    cT_sb = b.sb("cT_sb", [64, 8])
    dT_sb = b.sb("dT_sb", [64, 8])
    ft_sb = b.sb("ft_sb", [64, 8])
    s_sb = b.sb("s_sb", [128, 4, 128])
    s0t_sb = b.sb("s0t_sb", [128, 4, 128], BF16)
    zs_sb = b.sb("zs_sb", [128, 4, sl], BF16)
    a8 = b.sb("a8", [8, 3])
    ssq_row = b.sb("ssq_row", [1, sl])
    win = b.sb("win", [128, 10, KC * 128], BF16)
    wdt_sb = b.sb("wdt_sb", [128, KC, 8], BF16)
    wc_sb = b.sb("wc_sb", [128, 6, 4])
    cb_sb = b.sb("cb_sb", [128, 6])
    alog_sb = b.sb("alog_sb", [8, 1])
    dtb_sb = b.sb("dtb_sb", [8, 1])
    drep_sb = b.sb("drep_sb", [128, 4])
    sel8_sb = b.sb("sel8_sb", [8, 128])
    cm8_sb = b.sb("cm8_sb", [8, 4])
    maskT_sb = b.sb("maskT_sb", [64, 64])
    identb_sb = b.sb("identb_sb", [128, 128], BF16)
    identf_sb = b.sb("identf_sb", [128, 128])
    onesf = b.sb("onesf_sb", [128, 1])
    ones_r = b.sb("ones_r_sb", [1, 128])
    small = b.sb("small", [8, 8])

    if debug:
        dbg = {}
        for nm, shp, dt in [("u", [D, 512], BF16), ("z", [512, 512], BF16),
                            ("xc", [768, 512], BF16), ("dt", [8, 512], F32),
                            ("cum", [8, 512], F32), ("XT", [64, 512], BF16),
                            ("BT", [64, 128], BF16), ("mt", [64, 512], BF16),
                            ("ct", [128, 512], BF16), ("expd", [64, 512], F32),
                            ("s1", [128, 4 * 128], F32), ("y1", [512, 64], F32),
                            ("ft", [64, 8], F32), ("sdob", [128, 512], F32)]:
            dbg[nm] = b.out("dbg_" + nm, shp, dt)

    psA = b.ps("psA", [128, sl])
    psB = b.ps("psB", [128, sl])
    psC = b.ps("psC", [128, sl])
    psD = b.ps("psD", [128, 4, 128])
    psE = b.ps("psE", [128, 4, 128])
    psF = b.ps("psF", [128, 64])
    psG = b.ps("psG", [128, 64])
    psH = b.ps("psH", [64, 5, 128], BF16)
    psDf = psD[:, :, :].rearrange("p a b -> p (a b)")
    psEf = psE[:, :, :].rearrange("p a b -> p (a b)")

    p = b.start()
    csts = make_consts(b, p, [EPS, -0.5, 1.0])

    p.dma("sy", wdt_sb[:, :, :], wdt)
    p.dma("sy", wc_sb[:, :, :], wcv)
    p.dma("sy", cb_sb[:, :], cbv)
    p.dma("sy", alog_sb[:, :], alog)
    p.dma("sy", dtb_sb[:, :], dtb)
    p.dma("sy", drep_sb[:, :], drep)
    p.dma("sy", sel8_sb[:, :], sel8)
    p.dma("sy", cm8_sb[:, :], cmask8)
    p.dma("sy", maskT_sb[:, :], maskT_d)
    p.dma("sy", identb_sb[:, :], identb_d)
    p.dma("sy", identf_sb[:, :], identf_d)
    p.dma("sy", onesf[:, :], onesf_d)
    p.dma("sy", ones_r[:, :], ones_r_d)
    for i in range(4):
        p.dma("sy", win[:, i, :], wz[i])
        p.dma("sy", win[:, 4 + i, :], wx[i])
    for i in range(2):
        p.dma("sy", win[:, 8 + i, :], wbc[i])
    p.fence_dma("sy", "pe", "ve", "se")
    p.ve(lambda v: v.memset(z64[:, :], 0.0))
    p.ve(lambda v: v.memset(s_sb[:, :, :], 0.0))
    p.ve(lambda v: v.memset(xsp[:, :, :], 0.0))
    p.fence("ve", "pe", "se")
    p.se(lambda s: s.activation(out=a8[:, 0:1], in_=alog_sb[:, :], func=AF.Exp))
    p.fence("se", "ve")
    p.ve(lambda v: v.tensor_scalar_mul(a8[:, 0:1], a8[:, 0:1], -1.0))
    barrier(p)

    for slab in range(NSLAB):
        bidx, srel = slab // (NSLAB // B), slab % (NSLAB // B)
        t0 = bidx * L + srel * sl

        p.fence("ve", "sy")
        p.dma("sy", x_t[:, :, :], xTv[:, :, t0:t0 + sl])
        p.fence_dma("sy", "se", "pe", "ve")
        if slab > 0:
            p.fence("ve", "pe")
        emit_ln_rows(p, x_t, sq, KC, sl, onesf, psC, rows, csts)
        p.fence("se", "pe", "ve")
        p.pe(lambda t: t.matmul(psA[:, :], ones_r[:, :], rows[0:1, 0, :], start=True, stop=True))
        p.pe(lambda t: t.matmul(psB[:, :], ones_r[:, :], rows[0:1, 1, :], start=True, stop=True))
        p.fence("pe", "ve")
        mean_b = free_bcast(psA[:, 0:1], KC, sl)
        rstd_b = free_bcast(psB[:, 0:1], KC, sl)
        p.ve(lambda v: v.tensor_sub(sq[:, :, :], x_t[:, :, :], mean_b))
        p.ve(lambda v: v.tensor_mul(u[:, :, :], sq[:, :, :], rstd_b))
        barrier(p)

        if debug and slab == 0:
            p.fence("ve", "sy")
            p.dma("sy", dbg["u"].rearrange("(c p) t -> p c t", p=128), u[:, :, :])

        if srel == 0:
            p.ve(lambda v: v.memset(xsp[:, :, 0:3], 0.0))
            if slab > 0:
                p.ve(lambda v: v.memset(s_sb[:, :, :], 0.0))
        else:
            p.ve(lambda v: v.tensor_copy(xsp[:, :, 0:3], xsp[:, :, sl:sl + 3]))
        p.fence("ve", "se", "pe")

        pp = [psDf, psEf]
        for m in range(10):
            ps_ = pp[m % 2]
            for k in range(KC):
                p.pe(lambda t, k=k, m=m, ps_=ps_: t.matmul(
                    ps_[:, :], win[:, m, k * 128:(k + 1) * 128], u[:, k, :],
                    start=(k == 0), stop=(k == KC - 1)))
            p.fence("pe", "se")
            if m < 4:
                p.se(lambda s, m=m, ps_=ps_: s.copy(z_sb[:, m, :], ps_[:, :]))
            else:
                p.se(lambda s, m=m, ps_=ps_: s.copy(xsp[:, m - 4, 3:3 + sl], ps_[:, :]))
            p.fence("se", "pe")
        for k in range(KC):
            p.pe(lambda t, k=k: t.matmul(psC[0:8, 0:sl], wdt_sb[:, k, :], u[:, k, :],
                                         start=(k == 0), stop=(k == KC - 1)))
        barrier(p)

        for ti in range(6):
            p.ve(lambda v, ti=ti: v.tensor_scalar(
                out=cv[:, :], in0=xsp[:, ti, 0:sl], scalar1=wc_sb[:, ti, 0:1],
                scalar2=None, op0=ALU.mult))
            for j in range(1, 4):
                p.ve(lambda v, ti=ti, j=j: v.scalar_tensor_tensor(
                    out=cv[:, :], in0=xsp[:, ti, j:j + sl], scalar=wc_sb[:, ti, j:j + 1],
                    in1=cv[:, :], op0=ALU.mult, op1=ALU.add))
            p.fence("ve", "se")
            p.se(lambda s, ti=ti: s.activation(out=xc[:, ti, :], in_=cv[:, :],
                                               func=AF.Silu, bias=cb_sb[:, ti:ti + 1]))
            p.fence("se", "ve")
        barrier(p)

        if debug and slab == 0:
            p.fence("se", "sy")
            p.fence("ve", "sy")
            p.dma("sy", dbg["z"].rearrange("(c p) t -> p c t", p=128), z_sb[:, :, :])
            p.dma("sy", dbg["xc"].rearrange("(c p) t -> p c t", p=128), xc[:, :, :].rearrange("p c t -> p c t")[:, 0:6, :])

        p.ve(lambda v: v.tensor_scalar(out=dtr[:, 0, :], in0=psC[0:8, 0:sl],
                                       scalar1=dtb_sb[:, :], scalar2=None, op0=ALU.add))
        p.ve(lambda v: v.tensor_scalar(out=dtr[:, 1, :], in0=dtr[:, 0, :],
                                       scalar1=0.0, scalar2=None, op0=ALU.max))
        p.fence("ve", "se", "pe")
        p.se(lambda s: s.activation(out=dtr[:, 2, :], in_=dtr[:, 0, :], func=AF.Abs))
        p.se(lambda s: s.activation(out=dtr[:, 3, :], in_=dtr[:, 2, :], func=AF.Exp, scale=-1.0))
        p.se(lambda s: s.activation(out=dtr[:, 2, :], in_=dtr[:, 3, :], func=AF.Ln,
                                    bias=csts[1.0][0:8, :]))
        p.fence("se", "ve")
        p.ve(lambda v: v.tensor_add(dtr[:, 3, :], dtr[:, 1, :], dtr[:, 2, :]))
        p.ve(lambda v: v.tensor_scalar(out=dtr[:, 0, :], in0=dtr[:, 3, :],
                                       scalar1=a8[:, 0:1], scalar2=None, op0=ALU.mult))
        p.fence("ve", "ve")
        for cc in range(NCH):
            p.ve(lambda v, cc=cc: v.tensor_tensor_scan(
                cum[:, cc * 64:(cc + 1) * 64], dtr[:, 0, cc * 64:(cc + 1) * 64],
                z64[:, :], 0.0, ALU.add, ALU.add))
        p.fence("ve", "sy")
        p.dma("sy", cumd[:], cum[:, :])
        p.fence_dma("sy", "sy")
        p.dma("sy", cumflat[0:1, :], cumd[None, :])
        p.fence_dma("sy", "se", "pe", "ve")
        barrier(p)

        if debug and slab == 0:
            p.fence("ve", "sy")
            p.dma("sy", dbg["dt"], dtr[:, 3, :])
            p.dma("sy", dbg["cum"], cum[:, :])

        cfl = cumflat[0:1, :].rearrange("p (h t) -> p h t", t=sl)

        for cc in range(NCH):
            c0 = cc * 64
            for t in range(4):
                p.pe(lambda te, t=t, c0=c0: te.transpose(psH[:, t, :], xc[:, t, c0:c0 + 64], identb_sb[:, :]))
            p.pe(lambda te, c0=c0: te.transpose(psH[:, 4, :], xc[:, 4, c0:c0 + 64], identb_sb[:, :]))
            p.fence("pe", "se")
            p.se(lambda s: s.copy(XT_sb[:, :], psH[:, 0:4, :]))
            p.se(lambda s: s.copy(BT_sb[:, :], psH[:, 4, :]))
            p.fence("se", "pe", "ve")
            p.pe(lambda te, c0=c0: te.transpose(psC[0:64, 72:80], cum[:, c0:c0 + 64], identf_sb[0:8, 0:8]))
            p.pe(lambda te, c0=c0: te.transpose(psC[0:64, 80:88], dtr[:, 3, c0:c0 + 64], identf_sb[0:8, 0:8]))
            p.pe(lambda te, cc=cc: te.matmul(psC[0:64, 88:96], ones_r[0:1, 0:64],
                                             cfl[:, :, cc * 64 + 63], start=True, stop=True))
            p.fence("pe", "se", "ve")
            p.se(lambda s, cc=cc: s.activation(out=a8[:, 1:2], in_=cum[:, cc * 64 + 63:cc * 64 + 64], func=AF.Exp))
            p.fence("se", "ve")
            p.ve(lambda v: v.tensor_scalar(out=small[0:8, 0:4], in0=cm8_sb[:, :],
                                           scalar1=a8[:, 1:2], scalar2=None, op0=ALU.mult))
            p.ve(lambda v: v.tensor_copy(cT_sb[:, :], psC[0:64, 72:80]))
            p.ve(lambda v: v.tensor_copy(dT_sb[:, :], psC[0:64, 80:88]))
            p.fence("ve", "pe")
            p.pe(lambda te: te.matmul(psC[0:128, 96:100], sel8_sb[:, :], small[0:8, 0:4], start=True, stop=True))
            p.pe(lambda te, cc=cc: te.matmul(psA[0:64, 0:512], ones_r[0:1, 0:64],
                                             cfl[:, :, cc * 64:(cc + 1) * 64], start=True, stop=True))
            p.pe(lambda te, cc=cc: te.matmul(psB[:, 0:512], ones_r[:, :],
                                             cfl[:, :, cc * 64:(cc + 1) * 64], start=True, stop=True))
            p.pe(lambda te, c0=c0: te.matmul(psC[0:64, 0:64], xc[:, 4, c0:c0 + 64],
                                             xc[:, 5, c0:c0 + 64], start=True, stop=True))
            p.fence("pe", "ve", "se")
            p.se(lambda s: s.activation(out=psB[:, 0:512], in_=psB[:, 0:512], func=AF.Exp))
            p.fence("se", "ve")
            ctT_bc = bass.AP(cT_sb.ap().tensor, cT_sb.ap().offset,
                             [cT_sb.ap().ap[0], [1, 8], [0, 64]])
            p.ve(lambda v: v.scalar_tensor_tensor(out=dts[:, :], in0=ctT_bc, scalar=-1.0,
                                                  in1=psA[0:64, 0:512], op0=ALU.mult, op1=ALU.add))
            p.ve(lambda v: v.tensor_scalar(out=dts[:, :], in0=dts[:, :], scalar1=0.0,
                                           scalar2=None, op0=ALU.min))
            p.ve(lambda v: v.scalar_tensor_tensor(out=ft_sb[:, :], in0=cT_sb[:, :], scalar=-1.0,
                                                  in1=psC[0:64, 88:96], op0=ALU.mult, op1=ALU.add))
            p.ve(lambda v: v.tensor_mul(cbm[:, :], psC[0:64, 0:64], maskT_sb[:, :]))
            p.fence("ve", "se")
            p.se(lambda s: s.activation(out=dts[:, :], in_=dts[:, :], func=AF.Exp))
            p.se(lambda s: s.activation(out=ft_sb[:, :], in_=ft_sb[:, :], func=AF.Exp))
            p.fence("se", "ve")
            p.ve(lambda v: v.tensor_mul(ft_sb[:, :], ft_sb[:, :], dT_sb[:, :]))
            cbm_bc = bass.AP(cbm.ap().tensor, cbm.ap().offset, [cbm.ap().ap[0], [0, 8], [1, 64]])
            p.ve(lambda v: v.tensor_mul(dts[:, :], dts[:, :], cbm_bc))
            dT_bc = bass.AP(dT_sb.ap().tensor, dT_sb.ap().offset, [dT_sb.ap().ap[0], [1, 8], [0, 64]])
            p.ve(lambda v: v.tensor_mul(mt_sb[:, :], dts[:, :], dT_bc))
            ft_bc = bass.AP(ft_sb.ap().tensor, ft_sb.ap().offset, [ft_sb.ap().ap[0], [1, 8], [0, 64]])
            p.ve(lambda v: v.tensor_mul(xft[:, :], XT_sb[:, :], ft_bc))
            xcC_bc = bass.AP(xc.ap().tensor, xc.ap().offset + 5 * sl + c0,
                             [xc.ap().ap[0], [0, 8], [1, 64]])
            p.ve(lambda v: v.tensor_mul(ct_sb[:, :], xcC_bc, psB[:, 0:512]))
            p.fence("ve", "pe")
            for h in range(8):
                t = h // 2
                p.pe(lambda te, h=h, t=t: te.matmul(
                    psD[(h % 2) * 64:(h % 2) * 64 + 64, t, :],
                    xft[:, t * 128 + (h % 2) * 64: t * 128 + (h % 2) * 64 + 64],
                    BT_sb[:, :], start=True, stop=True))
            p.fence("pe", "ve")
            for t in range(4):
                p.ve(lambda v, t=t: v.scalar_tensor_tensor(
                    out=s_sb[:, t, :], in0=s_sb[:, t, :], scalar=psC[:, 96 + t:97 + t],
                    in1=psD[:, t, :], op0=ALU.mult, op1=ALU.add))
            p.fence("ve", "pe")
            for t in range(4):
                p.pe(lambda te, t=t: te.transpose(psE[:, t, :], s_sb[:, t, :], identf_sb[:, :]))
            p.fence("pe", "se")
            p.se(lambda s: s.copy(s0t_sb[:, :, :], psE[:, :, :]))
            p.fence("se", "pe")
            for t in range(4):
                bank = psF if t % 2 == 0 else psG
                for hh in range(2):
                    h = 2 * t + hh
                    p.pe(lambda te, h=h, t=t, hh=hh, bank=bank: te.matmul(
                        bank[hh * 64:hh * 64 + 64, :],
                        XT_sb[:, t * 128 + hh * 64: t * 128 + hh * 64 + 64],
                        mt_sb[:, h * 64:(h + 1) * 64], start=True, stop=False))
                    p.pe(lambda te, h=h, t=t, hh=hh, bank=bank: te.matmul(
                        bank[hh * 64:hh * 64 + 64, :],
                        s0t_sb[:, t, hh * 64:hh * 64 + 64],
                        ct_sb[:, h * 64:(h + 1) * 64], start=False, stop=True))
                p.fence("pe", "ve")
                p.ve(lambda v, t=t, c0=c0, bank=bank: v.scalar_tensor_tensor(
                    out=sq[:, t, c0:c0 + 64], in0=xc[:, t, c0:c0 + 64],
                    scalar=drep_sb[:, t:t + 1], in1=bank[:, :], op0=ALU.mult, op1=ALU.add))
                p.fence("ve", "pe")
            if debug and slab == 0 and cc == 0:
                p.fence("ve", "sy")
                p.fence("se", "sy")
                p.dma("sy", dbg["XT"], XT_sb[:, :])
                p.dma("sy", dbg["BT"], BT_sb[:, :])
                p.dma("sy", dbg["mt"], mt_sb[:, :])
                p.dma("sy", dbg["ct"], ct_sb[:, :])
                p.dma("sy", dbg["expd"], dts[:, :])
                p.dma("sy", dbg["ft"], ft_sb[:, :])
                p.ve(lambda v: v.tensor_copy(cv[:, :], psB[:, 0:512]))
                p.fence("ve", "sy")
                p.dma("sy", dbg["sdob"], cv[:, 0:512])
                p.dma("sy", dbg["s1"], s_sb[:, :, :].rearrange("p a b -> p (a b)"))
                p.dma("sy", dbg["y1"].rearrange("(c p) t -> p c t", p=128), sq[:, 0:4, 0:64])
                p.fence_dma("sy", "pe", "ve", "se")
            barrier(p)

        p.se(lambda s: s.activation(out=zs_sb[:, :, :], in_=z_sb[:, :, :], func=AF.Silu))
        p.fence("se", "ve")
        p.ve(lambda v: v.tensor_mul(sq[:, 4:8, :], sq[:, 0:4, :], zs_sb[:, :, :]))
        p.fence("ve", "se")
        p.se(lambda s: s.activation(out=sq[:, 8:12, :], in_=sq[:, 4:8, :], func=AF.Square))
        p.fence("se", "pe")
        for k in range(4):
            p.pe(lambda t, k=k: t.matmul(psC[0:1, 0:sl], onesf[:, :], sq[:, 8 + k, :],
                                         start=(k == 0), stop=(k == 3)))
        p.fence("pe", "se")
        p.se(lambda s: s.copy(ssq_row[:, :], psC[0:1, 0:sl]))
        p.fence("se", "sy")
        p.dma("sy", ygv[:, :, t0:t0 + sl], sq[:, 4:8, :])
        p.dma("sy", ssq[0:1, t0:t0 + sl], ssq_row[:, :])
        barrier(p)

    p.fence_dma("sy", "sy")
    return b.finish()


def host_att0(inputs, bw, core):
    inT = np.asarray(bw["L0_attn_inT"])
    cwf = np.asarray(inputs["L0_attn_conv_w"]).reshape(4352, 4)
    cbf = np.asarray(inputs["L0_attn_conv_b"])
    c = core
    zc = inT[:, c * 512:(c + 1) * 512]
    xcp = inT[:, 4096 + c * 512:4096 + (c + 1) * 512]
    bcc = inT[:, 8192:8448]
    dtc = inT[:, 8448 + 8 * c:8456 + 8 * c]
    wcl = np.concatenate([cwf[c * 512:(c + 1) * 512], cwf[4096:4352]], axis=0)
    cbl = np.concatenate([cbf[c * 512:(c + 1) * 512], cbf[4096:4352]])
    hs = slice(8 * c, 8 * c + 8)
    i64 = np.arange(64)
    sel = (np.arange(8)[:, None] % 2 == (np.arange(128)[None, :] // 64)).astype(np.float32)
    cm8 = (np.arange(8)[:, None] // 2 == np.arange(4)[None, :]).astype(np.float32)
    maskT = (i64[:, None] <= i64[None, :]).astype(np.float32)
    return dict(
        wz=stream_lhsT(zc), wx=stream_lhsT(xcp), wbc=stream_lhsT(bcc),
        wdt=cd(dtc.reshape(16, 128, 8).transpose(1, 0, 2)),
        wcv=cd(wcl.reshape(6, 128, 4).transpose(1, 0, 2).astype(np.float32)),
        cbv=tile128(cbl.astype(np.float32)),
        alog=cd(np.asarray(inputs["L0_attn_A_log"])[hs, None].astype(np.float32)),
        dtb=cd(np.asarray(inputs["L0_attn_dt_bias"])[hs, None].astype(np.float32)),
        drep=tile128(np.repeat(np.asarray(inputs["L0_attn_D"])[hs], 64).astype(np.float32)),
        sel8=sel, cmask8=cm8, maskT=maskT,
        identb=np.eye(128, dtype=BF), identf=np.eye(128, dtype=np.float32),
        onesf=np.ones((128, 1), np.float32), ones_r=np.ones((1, 128), np.float32),
    )



def build_att0fin(tok=TOK, tt=512):
    MI = 4096 // 128
    MO = D // 128
    NT = tok // tt

    b = Nc()
    xT = b.inp("xT", [D, tok])
    ygT = b.inp("ygT", [4096, tok])
    ssq8 = b.inp("ssq8", [8, tok])
    nw = b.inp("nw", [128, MI])
    wo = b.inp("wo", [MO, 128, MI * 128], BF16)
    onesf_d = b.inp("onesf", [128, 1])
    ones_r_d = b.inp("ones_r", [1, 128])
    yT = b.out("yT", [D, tok])

    xTv = xT.rearrange("(c p) t -> p c t", p=128)
    ygvv = ygT.rearrange("(c p) t -> p c t", p=128)
    yTv = yT.rearrange("(c p) t -> p c t", p=128)

    x_t = b.sb("x_t", [128, MO, tt])
    yg = b.sb("yg", [128, MI, tt])
    yn = b.sb("yn", [128, MI, tt], BF16)
    ss = b.sb("ss", [8, tt])
    rrow = b.sb("rrow", [1, 2, tt])
    res = b.sb("res", [128, 2, tt])
    nw_sb = b.sb("nw_sb", [128, MI])
    onesf = b.sb("onesf_sb", [128, 1])
    ones_r = b.sb("ones_r_sb", [1, 128])
    wB = b.sb("wB", [128, 2, MI * 128], BF16)
    psR = b.ps("psR", [128, tt])
    psO = [b.ps("psO0", [128, tt]), b.ps("psO1", [128, tt])]

    p = b.start()
    csts = make_consts(b, p, [EPS, -0.5, 1.0 / 4096.0])
    p.dma("sy", nw_sb[:, :], nw)
    p.dma("sy", onesf[:, :], onesf_d)
    p.dma("sy", ones_r[:, :], ones_r_d)
    p.fence_dma("sy", "pe", "ve", "se")

    wB_use = [0, 0]
    o_evac = 0
    res_store = [0, 0]
    for it in range(NT):
        t0 = it * tt
        p.fence("ve", "sy")
        p.dma("sy", x_t[:, :, :], xTv[:, :, t0:t0 + tt])
        p.dma("sy", yg[:, :, :], ygvv[:, :, t0:t0 + tt])
        p.dma("sy", ss[:, :], ssq8[:, t0:t0 + tt])
        p.fence_dma("sy", "pe", "ve", "se")
        if it > 0:
            p.fence("ve", "pe")
            p.fence("se", "pe")
        p.pe(lambda t: t.matmul(psR[0:1, 0:tt], onesf[0:8, :], ss[:, :], start=True, stop=True))
        p.fence("pe", "se")
        p.se(lambda s: s.activation(out=rrow[0:1, 0, :], in_=psR[0:1, 0:tt], func=AF.Ln,
                                    scale=csts[1.0 / 4096.0][0:1, :], bias=csts[EPS][0:1, :]))
        p.se(lambda s: s.activation(out=rrow[0:1, 1, :], in_=rrow[0:1, 0, :], func=AF.Exp,
                                    scale=csts[-0.5][0:1, :]))
        p.fence("se", "pe")
        p.pe(lambda t: t.matmul(psR[:, 0:tt], ones_r[:, :], rrow[0:1, 1, :], start=True, stop=True))
        p.fence("pe", "ve")
        rb = free_bcast(psR[:, 0:1], 1, tt)
        for m in range(MI):
            p.ve(lambda v, m=m: v.scalar_tensor_tensor(
                out=yn[:, m, :], in0=yg[:, m, :], scalar=nw_sb[:, m:m + 1],
                in1=psR[:, 0:tt], op0=ALU.mult, op1=ALU.mult))
        p.fence("ve", "pe", "gp")
        for mo in range(MO):
            buf = mo % 2
            p.wait("gp", "pe", wB_use[buf])
            ldv = p.dma("gp", wB[:, buf, :], wo[mo])
            p.wait_dma("pe", "gp", ldv)
            p.wait("pe", "ve", o_evac)
            for k in range(MI):
                p.pe(lambda t, k=k, buf=buf, mo=mo: t.matmul(
                    psO[mo % 2][:, :], wB[:, buf, k * 128:(k + 1) * 128], yn[:, k, :],
                    start=(k == 0), stop=(k == MI - 1)))
            wB_use[buf] = p.fence("pe", "ve")
            p.wait_dma("ve", "sy", res_store[buf])
            p.ve(lambda v, mo=mo, buf=buf: v.tensor_add(
                res[:, buf, :], psO[mo % 2][:, :], x_t[:, mo, :]))
            o_evac = p.fence("ve", "sy")
            res_store[buf] = p.dma("sy", yTv[:, mo, t0:t0 + tt], res[:, buf, :])
    p.fence_dma("sy", "sy")
    return b.finish()



def build_att1a(sl=512):
    KC = D // 128
    NSLAB = (B * L) // sl

    b = Nc()
    xT = b.inp("xT", [D, B * L])
    wh = b.inp("wh", [4, 128, KC * 128], BF16)
    wg = b.inp("wg", [4, 128, KC * 128], BF16)
    wxs = b.inp("wxs", [128, 4, 160])
    wcv = b.inp("wcv", [128, 4, 4])
    cbv = b.inp("cbv", [128, 4])
    onesf_d = b.inp("onesf", [128, 1])
    ones_r_d = b.inp("ones_r", [1, 128])
    hO = b.out("hO", [512, B * L], BF16)
    gO = b.out("gO", [512, B * L], BF16)
    pO = b.out("pO", [160, B * L])

    xTv = xT.rearrange("(c p) t -> p c t", p=128)
    hOv = hO.rearrange("(c p) t -> p c t", p=128)
    gOv = gO.rearrange("(c p) t -> p c t", p=128)

    x_t = b.sb("x_t", [128, KC, sl])
    sq = b.sb("sq", [128, KC, sl])
    u = b.sb("u", [128, KC, sl], BF16)
    rows = b.sb("rows", [1, 4, sl])
    hsp = b.sb("hsp", [128, 4, sl + 3], BF16)
    g_sb = b.sb("g_sb", [128, 4, sl], BF16)
    cv = b.sb("cv", [128, sl])
    hc = b.sb("hc", [128, 4, sl], BF16)
    po1 = b.sb("po1", [128, sl])
    po2 = b.sb("po2", [32, sl])
    win_h = b.sb("win_h", [128, 4, KC * 128], BF16)
    win_g = b.sb("win_g", [128, 4, KC * 128], BF16)
    wx_sb = b.sb("wx_sb", [128, 4, 160], BF16)
    wc_sb = b.sb("wc_sb", [128, 4, 4])
    cb_sb = b.sb("cb_sb", [128, 4])
    onesf = b.sb("onesf_sb", [128, 1])
    ones_r = b.sb("ones_r_sb", [1, 128])

    psA = b.ps("psA", [128, sl])
    psB = b.ps("psB", [128, sl])
    psC = b.ps("psC", [128, sl])
    psD = b.ps("psD", [128, sl])
    psE = b.ps("psE", [128, sl])

    p = b.start()
    csts = make_consts(b, p, [EPS, -0.5])
    for i in range(4):
        p.dma("sy", win_h[:, i, :], wh[i])
        p.dma("sy", win_g[:, i, :], wg[i])
    p.dma("gp", wx_sb[:, :, :], wxs)
    p.dma("sy", wc_sb[:, :, :], wcv)
    p.dma("sy", cb_sb[:, :], cbv)
    p.dma("sy", onesf[:, :], onesf_d)
    p.dma("sy", ones_r[:, :], ones_r_d)
    p.fence_dma("sy", "pe", "ve", "se")
    p.fence_dma("gp", "pe", "ve", "se")
    p.ve(lambda v: v.memset(hsp[:, :, :], 0.0))
    barrier(p)

    pp = [psD, psE]
    for slab in range(NSLAB):
        bidx, srel = slab // (NSLAB // B), slab % (NSLAB // B)
        t0 = bidx * L + srel * sl

        p.fence("ve", "sy")
        p.fence("se", "sy")
        p.dma("sy", x_t[:, :, :], xTv[:, :, t0:t0 + sl])
        p.fence_dma("sy", "se", "pe", "ve")
        if slab > 0:
            p.fence("ve", "pe")
        emit_ln_rows(p, x_t, sq, KC, sl, onesf, psC, rows, csts)
        p.fence("se", "pe", "ve")
        p.pe(lambda t: t.matmul(psA[:, :], ones_r[:, :], rows[0:1, 0, :], start=True, stop=True))
        p.pe(lambda t: t.matmul(psB[:, :], ones_r[:, :], rows[0:1, 1, :], start=True, stop=True))
        p.fence("pe", "ve")
        p.ve(lambda v: v.tensor_sub(sq[:, :, :], x_t[:, :, :], free_bcast(psA[:, 0:1], KC, sl)))
        p.ve(lambda v: v.tensor_mul(u[:, :, :], sq[:, :, :], free_bcast(psB[:, 0:1], KC, sl)))
        barrier(p)

        if srel == 0:
            p.ve(lambda v: v.memset(hsp[:, :, 0:3], 0.0))
        else:
            p.ve(lambda v: v.tensor_copy(hsp[:, :, 0:3], hsp[:, :, sl:sl + 3]))
        p.fence("ve", "se", "pe")

        for m in range(8):
            ps_ = pp[m % 2]
            wsrc = win_h if m < 4 else win_g
            mi = m % 4
            for k in range(KC):
                p.pe(lambda t, k=k, mi=mi, ps_=ps_, wsrc=wsrc: t.matmul(
                    ps_[:, :], wsrc[:, mi, k * 128:(k + 1) * 128], u[:, k, :],
                    start=(k == 0), stop=(k == KC - 1)))
            p.fence("pe", "se")
            if m < 4:
                p.se(lambda s, mi=mi, ps_=ps_: s.copy(hsp[:, mi, 3:3 + sl], ps_[:, :]))
            else:
                p.se(lambda s, mi=mi, ps_=ps_: s.copy(g_sb[:, mi, :], ps_[:, :]))
            p.fence("se", "pe")
        barrier(p)

        for ti in range(4):
            p.ve(lambda v, ti=ti: v.tensor_scalar(
                out=cv[:, :], in0=hsp[:, ti, 0:sl], scalar1=wc_sb[:, ti, 0:1],
                scalar2=None, op0=ALU.mult))
            for j in range(1, 4):
                p.ve(lambda v, ti=ti, j=j: v.scalar_tensor_tensor(
                    out=cv[:, :], in0=hsp[:, ti, j:j + sl], scalar=wc_sb[:, ti, j:j + 1],
                    in1=cv[:, :], op0=ALU.mult, op1=ALU.add))
            p.fence("ve", "se")
            p.se(lambda s, ti=ti: s.activation(out=hc[:, ti, :], in_=cv[:, :],
                                               func=AF.Silu, bias=cb_sb[:, ti:ti + 1]))
            p.fence("se", "ve", "pe")

        p.wait("pe", "ve", 0)
        for k in range(4):
            p.pe(lambda t, k=k: t.matmul(psA[:, :], wx_sb[:, k, 0:128], hc[:, k, :],
                                         start=(k == 0), stop=(k == 3)))
        for k in range(4):
            p.pe(lambda t, k=k: t.matmul(psB[0:32, 0:sl], wx_sb[:, k, 128:160], hc[:, k, :],
                                         start=(k == 0), stop=(k == 3)))
        p.fence("pe", "ve")
        p.ve(lambda v: v.tensor_copy(po1[:, :], psA[:, :]))
        p.ve(lambda v: v.tensor_copy(po2[:, :], psB[0:32, 0:sl]))
        p.fence("ve", "sy", "pe")
        p.dma("sy", hOv[:, :, t0:t0 + sl], hc[:, :, :])
        p.dma("sy", gOv[:, :, t0:t0 + sl], g_sb[:, :, :])
        p.dma("sy", pO[0:128, t0:t0 + sl], po1[:, :])
        p.dma("sy", pO[128:160, t0:t0 + sl], po2[:, :])
        barrier(p)

    p.fence_dma("sy", "sy")
    return b.finish()


def host_att1a(inputs, bw, core):
    winT = np.asarray(bw["L1_attn_W_inT"])
    c = core
    wh = stream_lhsT(winT[:, c * 512:(c + 1) * 512])
    wg = stream_lhsT(winT[:, 4096 + c * 512:4096 + (c + 1) * 512])
    wxT = np.asarray(inputs["L1_attn_Wx"]).T.astype(np.float32)
    wxs = cd(wxT[c * 512:(c + 1) * 512].reshape(4, 128, 160).transpose(1, 0, 2))
    cw = np.asarray(inputs["L1_attn_conv_w"]).astype(np.float32)[c * 512:(c + 1) * 512]
    cb = np.asarray(inputs["L1_attn_conv_b"]).astype(np.float32)[c * 512:(c + 1) * 512]
    return dict(
        wh=wh, wg=wg, wxs=wxs,
        wcv=cd(cw.reshape(4, 128, 4).transpose(1, 0, 2)),
        cbv=tile128(cb),
        onesf=np.ones((128, 1), np.float32), ones_r=np.ones((1, 128), np.float32),
    )



def build_att1b(sl=512, sc=256, debug=False):
    NSLAB = (B * L) // sl
    NSC = sl // sc

    b = Nc()
    hI = b.inp("hI", [512, B * L], BF16)
    gI = b.inp("gI", [512, B * L], BF16)
    pI = b.inp("pI", [8, 160, B * L])
    wdts = b.inp("wdts", [4, 128, 128], BF16)
    a_in = b.inp("a_in", [128, 4, 16])
    dtb = b.inp("dtb", [128, 4])
    dv_in = b.inp("dv_in", [128, 4])
    lnw_dt = b.inp("lnw_dt", [128, 1])
    lnw_bc = b.inp("lnw_bc", [32, 1])
    bsel = b.inp("bsel", [32, 2])
    bsel2 = b.inp("bsel2", [2, 32])
    wo = b.inp("wo", [16, 128, 512], BF16)
    onesf_d = b.inp("onesf", [128, 1])
    ones_r_d = b.inp("ones_r", [1, 128])
    oO = b.out("oO", [D, B * L])
    bcd = b.scratch("bcd", [32 * 512], BF16)

    hIv = hI.rearrange("(c p) t -> p c t", p=128)
    gIv = gI.rearrange("(c p) t -> p c t", p=128)
    oOv = oO.rearrange("(c p) t -> p c t", p=128)

    h_sb = b.sb("h_sb", [128, 4, sl], BF16)
    g_sb = b.sb("g_sb", [128, 4, sl], BF16)
    pp1 = b.sb("pp1", [128, 8, sl])
    pp2 = b.sb("pp2", [32, 8, sl])
    tsn = b.sb("tsn", [128, sl], BF16)
    bcn = b.sb("bcn", [32, sl], BF16)
    sq1 = b.sb("sq1", [128, sl])
    rrow = b.sb("rrow", [1, 2, sl])
    rrow2 = b.sb("rrow2", [2, 2, sl])
    delta = b.sb("delta", [128, 4, sl])
    brep = b.sb("brep", [128, 16 * sl], BF16)
    crep = b.sb("crep", [128, 16 * sl], BF16)
    dA = b.sb("dA", [128, 16 * sc])
    dBu = b.sb("dBu", [128, 16 * sc])
    s_scan = b.sb("s_scan", [128, 16 * sc])
    du = b.sb("du", [128, sc])
    tr1 = b.sb("tr1", [128, 8 * sc])
    state = b.sb("state", [128, 64])
    y_sb = b.sb("y_sb", [128, 4, sl])
    yn = b.sb("yn", [128, 4, sl], BF16)
    oo = b.sb("oo", [128, 2, sl])
    wdt_sb = b.sb("wdt_sb", [128, 4, 128], BF16)
    a_sb = b.sb("a_sb", [128, 4, 16])
    dtb_sb = b.sb("dtb_sb", [128, 4])
    dv_sb = b.sb("dv_sb", [128, 4])
    lnwd_sb = b.sb("lnwd_sb", [128, 1])
    lnwbc_sb = b.sb("lnwbc_sb", [32, 1])
    bsel_sb = b.sb("bsel_sb", [32, 2])
    bsel2_sb = b.sb("bsel2_sb", [2, 32])
    wo_sb = b.sb("wo_sb", [128, 2, 512], BF16)
    onesf = b.sb("onesf_sb", [128, 1])
    ones_r = b.sb("ones_r_sb", [1, 128])

    psA = b.ps("psA", [128, sl])
    psB = b.ps("psB", [128, sl])
    psC = b.ps("psC", [128, sl])
    psD = b.ps("psD", [128, sl])

    if debug:
        dbg = {}
        for nm, shp, dt in [("tsn", [128, sl], BF16), ("bcn", [32, sl], BF16),
                            ("delta", [512, sl], F32), ("dA", [128, 16 * sc], F32),
                            ("dBu", [128, 16 * sc], F32), ("sscan", [128, 16 * sc], F32),
                            ("y", [512, sl], F32), ("brep", [128, 16 * sl], BF16), ("crep", [128, 16 * sl], BF16), ("sc_", [128, 16 * sc], F32), ("tr", [128, sc], F32)]:
            dbg[nm] = b.out("dbg_" + nm, shp, dt)

    p = b.start()
    csts = make_consts(b, p, [EPS, -0.5, 1.0, 1.0 / 128.0, 1.0 / 16.0])
    for m in range(4):
        p.dma("sy", wdt_sb[:, m, :], wdts[m])
    p.dma("sy", a_sb[:, :, :], a_in)
    p.dma("sy", dtb_sb[:, :], dtb)
    p.dma("sy", dv_sb[:, :], dv_in)
    p.dma("sy", lnwd_sb[:, :], lnw_dt)
    p.dma("sy", lnwbc_sb[:, :], lnw_bc)
    p.dma("sy", bsel_sb[:, :], bsel)
    p.dma("sy", bsel2_sb[:, :], bsel2)
    p.dma("sy", onesf[:, :], onesf_d)
    p.dma("sy", ones_r[:, :], ones_r_d)
    p.fence_dma("sy", "pe", "ve", "se")
    p.ve(lambda v: v.memset(state[:, :], 0.0))
    barrier(p)

    wo_use = [0, 0]
    oo_store = [0, 0]
    dt_evac = [0, 0]
    wo_evac = [0, 0]
    dab_free = [0, 0]
    for slab in range(NSLAB):
        bidx, srel = slab // (NSLAB // B), slab % (NSLAB // B)
        t0 = bidx * L + srel * sl
        if srel == 0 and slab > 0:
            p.ve(lambda v: v.memset(state[:, :], 0.0))
            p.fence("ve", "ve")

        p.fence("ve", "sy")
        p.fence("se", "sy")
        p.dma("sy", h_sb[:, :, :], hIv[:, :, t0:t0 + sl])
        p.dma("sy", g_sb[:, :, :], gIv[:, :, t0:t0 + sl])
        for j in range(8):
            p.dma("sy", pp1[:, j, :], pI[j, 0:128, t0:t0 + sl])
            p.dma("sy", pp2[:, j, :], pI[j, 128:160, t0:t0 + sl])
        p.fence_dma("sy", "ve", "se", "pe")

        for st_, n_ in ((pp1, 128), (pp2, 32)):
            for half in (4, 2, 1):
                p.ve(lambda v, st_=st_, n_=n_, half=half: v.tensor_add(
                    st_[0:n_, 0:half, :], st_[0:n_, 0:half, :], st_[0:n_, half:2 * half, :]))
                p.mark("ve")
        p.fence("ve", "se", "pe")

        p.se(lambda s: s.activation(out=sq1[:, :], in_=pp1[:, 0, :], func=AF.Square))
        p.fence("se", "pe")
        p.pe(lambda t: t.matmul(psA[0:1, 0:sl], onesf[:, :], sq1[:, :], start=True, stop=True))
        p.fence("pe", "se")
        p.se(lambda s: s.activation(out=rrow[0:1, 0, :], in_=psA[0:1, 0:sl], func=AF.Ln,
                                    scale=csts[1.0 / 128.0][0:1, :], bias=csts[EPS][0:1, :]))
        p.se(lambda s: s.activation(out=rrow[0:1, 1, :], in_=rrow[0:1, 0, :], func=AF.Exp,
                                    scale=csts[-0.5][0:1, :]))
        p.fence("se", "pe")
        p.pe(lambda t: t.matmul(psA[:, :], ones_r[:, :], rrow[0:1, 1, :], start=True, stop=True))
        p.fence("pe", "ve")
        p.ve(lambda v: v.scalar_tensor_tensor(out=tsn[:, :], in0=pp1[:, 0, :],
                                              scalar=lnwd_sb[:, :], in1=psA[:, :],
                                              op0=ALU.mult, op1=ALU.mult))
        p.fence("ve", "se")
        p.se(lambda s: s.activation(out=sq1[0:32, 0:sl], in_=pp2[:, 0, :], func=AF.Square))
        p.fence("se", "pe")
        p.pe(lambda t: t.matmul(psB[0:2, 0:sl], bsel_sb[:, :], sq1[0:32, 0:sl], start=True, stop=True))
        p.fence("pe", "se")
        p.se(lambda s: s.activation(out=rrow2[:, 0, :], in_=psB[0:2, 0:sl], func=AF.Ln,
                                    scale=csts[1.0 / 16.0][0:2, :], bias=csts[EPS][0:2, :]))
        p.se(lambda s: s.activation(out=rrow2[:, 1, :], in_=rrow2[:, 0, :], func=AF.Exp,
                                    scale=csts[-0.5][0:2, :]))
        p.fence("se", "pe")
        p.pe(lambda t: t.matmul(psB[0:32, 0:sl], bsel2_sb[:, :], rrow2[:, 1, :], start=True, stop=True))
        p.fence("pe", "ve")
        p.ve(lambda v: v.scalar_tensor_tensor(out=bcn[:, :], in0=pp2[:, 0, :],
                                              scalar=lnwbc_sb[:, :], in1=psB[0:32, 0:sl],
                                              op0=ALU.mult, op1=ALU.mult))
        p.fence("ve", "sy", "pe")
        p.dma("sy", bcd[:], bcn[:, :])
        p.fence_dma("sy", "sy")
        p.dma("sy", brep[:, :], bass.AP(bcd.tensor, 0, [[0, 128], [512, 16], [1, 512]]))
        p.dma("sy", crep[:, :], bass.AP(bcd.tensor, 16 * 512, [[0, 128], [512, 16], [1, 512]]))

        for m in range(4):
            ps_ = [psC, psD][m % 2]
            p.wait("pe", "ve", max(dt_evac[m % 2], wo_evac[m % 2]))
            p.pe(lambda t, m=m, ps_=ps_: t.matmul(ps_[:, :], wdt_sb[:, m, :], tsn[:, :],
                                                  start=True, stop=True))
            p.fence("pe", "ve")
            p.ve(lambda v, m=m, ps_=ps_: v.tensor_scalar(
                out=delta[:, m, :], in0=ps_[:, :], scalar1=dtb_sb[:, m:m + 1],
                scalar2=None, op0=ALU.add))
            dt_evac[m % 2] = p.mark("ve")
        p.fence("ve", "se")
        dfl = delta[:, :, :].rearrange("p a t -> p (a t)")
        sfl = y_sb[:, :, :].rearrange("p a t -> p (a t)")
        p.ve(lambda v: v.tensor_scalar(out=sfl, in0=dfl, scalar1=0.0, scalar2=None, op0=ALU.max))
        p.fence("ve", "se")
        p.se(lambda s: s.activation(out=dfl, in_=dfl, func=AF.Abs))
        p.se(lambda s: s.activation(out=dfl, in_=dfl, func=AF.Exp, scale=-1.0))
        p.se(lambda s: s.activation(out=dfl, in_=dfl, func=AF.Ln, bias=csts[1.0][:, :]))
        p.fence("se", "ve")
        p.ve(lambda v: v.tensor_add(dfl, dfl, sfl))
        p.fence("ve", "se")
        p.fence_dma("sy", "ve")

        if debug and slab == 0:
            p.fence("ve", "sy")
            p.fence("se", "sy")
            p.dma("sy", dbg["tsn"], tsn[:, :])
            p.dma("sy", dbg["bcn"], bcn[:, :])
            p.dma("sy", dbg["delta"].rearrange("(a p) t -> p a t", p=128), delta[:, :, :])
            p.dma("sy", dbg["brep"], brep[:, :])
            p.dma("sy", dbg["crep"], crep[:, :])

        for it in range(4):
            for s2 in range(NSC):
                c0 = s2 * sc
                for n in range(16):
                    p.se(lambda s, it=it, n=n, c0=c0: s.activation(
                        out=dA[:, n * sc:(n + 1) * sc], in_=delta[:, it, c0:c0 + sc],
                        func=AF.Exp, scale=a_sb[:, it, n:n + 1]))
                p.ve(lambda v, it=it, c0=c0: v.tensor_mul(
                    du[:, :], delta[:, it, c0:c0 + sc], h_sb[:, it, c0:c0 + sc]))
                du_bc = bass.AP(du.ap().tensor, du.ap().offset, [du.ap().ap[0], [0, 16], [1, sc]])
                br_sl = bass.AP(brep.ap().tensor, brep.ap().offset + c0,
                                [brep.ap().ap[0], [sl, 16], [1, sc]])
                cr_sl = bass.AP(crep.ap().tensor, crep.ap().offset + c0,
                                [crep.ap().ap[0], [sl, 16], [1, sc]])
                p.ve(lambda v, du_bc=du_bc, br_sl=br_sl: v.tensor_mul(dBu[:, :], du_bc, br_sl))
                p.fence("se", "ve")
                for n in range(16):
                    p.ve(lambda v, n=n, it=it: v.tensor_tensor_scan(
                        s_scan[:, n * sc:(n + 1) * sc], dA[:, n * sc:(n + 1) * sc],
                        dBu[:, n * sc:(n + 1) * sc], state[:, it * 16 + n:it * 16 + n + 1],
                        ALU.mult, ALU.add))
                last_bc = bass.AP(s_scan.ap().tensor, s_scan.ap().offset + sc - 1,
                                  [s_scan.ap().ap[0], [sc, 16]])
                p.ve(lambda v, it=it, last_bc=last_bc: v.tensor_copy(
                    state[:, it * 16:(it + 1) * 16], last_bc))
                p.ve(lambda v, cr_sl=cr_sl: v.tensor_mul(dA[:, :], s_scan[:, :], cr_sl))
                p.ve(lambda v: v.tensor_add(tr1[:, :], dA[:, 0:8 * sc], dA[:, 8 * sc:16 * sc]))
                p.mark("ve")
                p.ve(lambda v: v.tensor_add(tr1[:, 0:4 * sc], tr1[:, 0:4 * sc], tr1[:, 4 * sc:8 * sc]))
                p.mark("ve")
                p.ve(lambda v: v.tensor_add(tr1[:, 0:2 * sc], tr1[:, 0:2 * sc], tr1[:, 2 * sc:4 * sc]))
                p.mark("ve")
                p.ve(lambda v: v.tensor_add(tr1[:, 0:sc], tr1[:, 0:sc], tr1[:, sc:2 * sc]))
                if debug and slab == 0 and it == 0 and s2 == 0:
                    p.fence("ve", "sy")
                    p.dma("sy", dbg["tr"], tr1[:, 0:sc])
                    p.fence_dma("sy", "ve")
                p.mark("ve")
                p.ve(lambda v, it=it, c0=c0: v.scalar_tensor_tensor(
                    out=y_sb[:, it, c0:c0 + sc], in0=h_sb[:, it, c0:c0 + sc],
                    scalar=dv_sb[:, it:it + 1], in1=tr1[:, 0:sc],
                    op0=ALU.mult, op1=ALU.add))
                p.fence("ve", "se")
        if debug and slab == 0:
            p.fence("ve", "sy")
            p.dma("sy", dbg["y"].rearrange("(a p) t -> p a t", p=128), y_sb[:, :, :])
        p.fence("ve", "se")
        p.se(lambda s: s.activation(out=yn[:, :, :], in_=g_sb[:, :, :], func=AF.Silu))
        p.fence("se", "ve")
        p.ve(lambda v: v.tensor_mul(y_sb[:, :, :], y_sb[:, :, :], yn[:, :, :]))
        p.ve(lambda v: v.tensor_copy(yn[:, :, :], y_sb[:, :, :]))
        p.fence("ve", "pe", "gp")

        for mo in range(16):
            buf = mo % 2
            p.wait("gp", "pe", wo_use[buf])
            ldv = p.dma("gp", wo_sb[:, buf, :], wo[mo])
            p.wait_dma("pe", "gp", ldv)
            ps_ = [psC, psD][mo % 2]
            p.wait("pe", "ve", max(wo_evac[mo % 2], dt_evac[mo % 2]))
            for k in range(4):
                p.pe(lambda t, k=k, buf=buf, ps_=ps_: t.matmul(
                    ps_[:, :], wo_sb[:, buf, k * 128:(k + 1) * 128], yn[:, k, :],
                    start=(k == 0), stop=(k == 3)))
            wo_use[buf] = p.fence("pe", "ve")
            p.wait_dma("ve", "sy", oo_store[buf])
            p.ve(lambda v, buf=buf, ps_=ps_: v.tensor_copy(oo[:, buf, :], ps_[:, :]))
            wo_evac[mo % 2] = p.fence("ve", "sy")
            oo_store[buf] = p.dma("sy", oOv[:, mo, t0:t0 + sl], oo[:, buf, :])
        barrier(p)

    p.fence_dma("sy", "sy")
    return b.finish()


def host_att1b(inputs, bw, core):
    c = core
    wdtT = np.asarray(bw["L1_attn_WdtT"])
    wdts = stream_lhsT(wdtT[:, c * 512:(c + 1) * 512])
    woT = np.asarray(bw["L1_attn_WoutT"])
    wo = stream_lhsT(woT[c * 512:(c + 1) * 512, :])
    isl = slice(c * 512, (c + 1) * 512)
    A = np.asarray(inputs["L1_attn_A"]).astype(np.float32)[isl]
    bsel = (np.arange(32)[:, None] // 16 == np.arange(2)[None, :]).astype(np.float32)
    return dict(
        wdts=wdts, wo=wo,
        a_in=cd(A.reshape(4, 128, 16).transpose(1, 0, 2)),
        dtb=tile128(np.asarray(inputs["L1_attn_dt_bias"]).astype(np.float32)[isl]),
        dv_in=tile128(np.asarray(inputs["L1_attn_D"]).astype(np.float32)[isl]),
        lnw_dt=np.asarray(inputs["L1_attn_dt_ln_w"]).astype(np.float32)[:, None],
        lnw_bc=np.concatenate([np.asarray(inputs["L1_attn_b_ln_w"]),
                               np.asarray(inputs["L1_attn_c_ln_w"])]).astype(np.float32)[:, None],
        bsel=bsel, bsel2=cdT(bsel),
        onesf=np.ones((128, 1), np.float32), ones_r=np.ones((1, 128), np.float32),
    )



def build_att1c(tok=TOK, tt=512):
    MO = D // 128
    NT = tok // tt
    b = Nc()
    xT = b.inp("xT", [D, tok])
    pO = b.inp("pO", [8, D, tok])
    yT = b.out("yT", [D, tok])
    xTv = xT.rearrange("(c p) t -> p c t", p=128)
    yTv = yT.rearrange("(c p) t -> p c t", p=128)

    x_t = b.sb("x_t", [128, 2, tt])
    acc = b.sb("acc", [128, 2, 8, tt])
    res = b.sb("res", [128, 2, tt])
    p = b.start()
    pOv = [pO[j].rearrange("(c p) t -> p c t", p=128) for j in range(8)]
    st_done = [0, 0]
    for it in range(NT):
        t0 = it * tt
        for mo in range(MO):
            buf = mo % 2
            p.fence("ve", "sy")
            p.dma("sy", x_t[:, buf, :], xTv[:, mo, t0:t0 + tt])
            for j in range(8):
                p.dma("sy", acc[:, buf, j, :], pOv[j][:, mo, t0:t0 + tt])
            p.fence_dma("sy", "ve")
            for half in (4, 2, 1):
                p.ve(lambda v, half=half, buf=buf: v.tensor_add(
                    acc[:, buf, 0:half, :], acc[:, buf, 0:half, :],
                    acc[:, buf, half:2 * half, :]))
                p.mark("ve")
            p.wait_dma("ve", "sy", st_done[buf])
            p.ve(lambda v, buf=buf: v.tensor_add(res[:, buf, :], acc[:, buf, 0, :], x_t[:, buf, :]))
            p.fence("ve", "sy")
            st_done[buf] = p.dma("sy", yTv[:, mo, t0:t0 + tt], res[:, buf, :])
    p.fence_dma("sy", "sy")
    return b.finish()



def _to_featmajor(shards):
    full = np.empty((D, B * L), np.float32)
    for c in range(NCORES):
        bidx = c // (NCORES // B)
        s = (c % (NCORES // B)) * TOK
        full[:, bidx * L + s:bidx * L + s + TOK] = shards[c]
    return full


def _fm_slices(full):
    out = []
    for c in range(NCORES):
        bidx = c // (NCORES // B)
        s = (c % (NCORES // B)) * TOK
        out.append(np.ascontiguousarray(full[:, bidx * L + s:bidx * L + s + TOK]))
    return out


def kernel(**inputs):
    inp = inputs
    bw = run_prep(inp)

    x0 = np.asarray(inp["L0_attn_x"])
    shards = shard_tokens(x0)

    nc_ffn0 = build_ffn0()
    base0 = host_ffn0(inp, bw, "L0_ffn_")
    res = run_launch(nc_ffn0, [dict(base0, xT=shards[c]) for c in range(NCORES)])
    x1s = [r["yT"] for r in res.results]
    x1T = _to_featmajor(x1s)

    nc_att0 = build_att0()
    res = run_launch(nc_att0, [dict(host_att0(inp, bw, c), xT=x1T) for c in range(NCORES)])
    yg = np.concatenate([r["ygT"] for r in res.results], axis=0)
    ssq = np.concatenate([r["ssq"] for r in res.results], axis=0)

    nc_fin = build_att0fin()
    basef = dict(
        nw=tile128(np.asarray(inp["L0_attn_norm_w"]).astype(np.float32)),
        wo=bw["L0_attn_outT"],
        onesf=np.ones((128, 1), np.float32), ones_r=np.ones((1, 128), np.float32),
    )
    in_maps = []
    for c in range(NCORES):
        bidx = c // (NCORES // B)
        s = (c % (NCORES // B)) * TOK
        cols = slice(bidx * L + s, bidx * L + s + TOK)
        in_maps.append(dict(basef, xT=np.ascontiguousarray(x1T[:, cols]),
                            ygT=np.ascontiguousarray(yg[:, cols]),
                            ssq8=np.ascontiguousarray(ssq[:, cols])))
    res = run_launch(nc_fin, in_maps)
    x2s = [r["yT"] for r in res.results]

    res = run_launch(nc_ffn0, [dict(base0, xT=x2s[c]) for c in range(NCORES)])
    x3s = [r["yT"] for r in res.results]

    nc_ffn1 = build_ffn1()
    base1 = host_ffn1(inp, bw)
    res = run_launch(nc_ffn1, [dict(base1, xT=x3s[c]) for c in range(NCORES)])
    x4s = [r["yT"] for r in res.results]
    x4T = _to_featmajor(x4s)

    nc_a1 = build_att1a()
    res = run_launch(nc_a1, [dict(host_att1a(inp, bw, c), xT=x4T) for c in range(NCORES)])
    pD = np.stack([r["pO"] for r in res.results])
    nc_b1 = build_att1b()
    res2 = run_launch(nc_b1, [dict(host_att1b(inp, bw, c), hI=res.results[c]["hO"],
                                   gI=res.results[c]["gO"], pI=pD)
                              for c in range(NCORES)])
    oD = np.stack([r["oO"] for r in res2.results])
    nc_c1 = build_att1c()
    in_maps = []
    for c in range(NCORES):
        bidx = c // (NCORES // B)
        s = (c % (NCORES // B)) * TOK
        cols = slice(bidx * L + s, bidx * L + s + TOK)
        in_maps.append(dict(xT=x4s[c], pO=np.ascontiguousarray(oD[:, :, cols])))
    res = run_launch(nc_c1, in_maps)
    x5s = [r["yT"] for r in res.results]

    res = run_launch(nc_ffn1, [dict(base1, xT=x5s[c]) for c in range(NCORES)])
    x6s = [r["yT"] for r in res.results]

    return unshard_tokens(x6s)
